# version 28
# baseline (speedup 1.0000x reference)
import contextlib

import numpy as np
import ml_dtypes

import concourse.bass as bass
import concourse.mybir as mybir
from concourse.bass_utils import run_bass_kernel_spmd

F32 = mybir.dt.float32
BF16 = mybir.dt.bfloat16
AF = mybir.ActivationFunctionType
ALU = mybir.AluOpType
AX = mybir.AxisListType.X

H = 1024
HC = 8
TQ = 2048
NQT = 16
NQG = 4
TKV = 4096
NKT = 32
NKG = 8
FF = 4096
NFC = 32

ENGINES = ("sync", "gpsimd", "vector", "scalar", "tensor")
NDMA = 8

SEM_NAMES = (["tS", "vS", "sS", "gS"]
             + [f"dSy{i}" for i in range(NDMA)]
             + [f"dGp{i}" for i in range(NDMA)])


class Em:

    def __init__(self):
        self.streams = {e: [] for e in ENGINES}
        self.counts = {n: 0 for n in SEM_NAMES}
        self.qcount = {"dSy": 0, "dGp": 0}

    def ev(self, name):
        return (name, self.counts[name])

    def op(self, eng, fn, waits=(), inc=None):
        if inc in ("dSy", "dGp"):
            pool = inc
            idx = self.qcount[pool]
            self.qcount[pool] += 1
            s = f"{pool}{idx % NDMA}"
            prior = idx // NDMA
            waits = list(waits) + [(s, 16 * prior)]
            self.counts[s] += 16
            ws = tuple((sn, v) for (sn, v) in waits if sn is not None and v > 0)
            self.streams[eng].append((ws, fn, s))
            return (s, self.counts[s])
        ws = tuple((sn, v) for (sn, v) in waits if sn is not None and v > 0)
        if inc is not None:
            self.counts[inc] += 1
        self.streams[eng].append((ws, fn, inc))
        return self.ev(inc) if inc else None

    def replay(self, block, sems):
        for eng_name in ENGINES:
            stream = self.streams[eng_name]

            def body(e, stream=stream):
                wm = {}
                for ws, fn, inc in stream:
                    for (s, v) in ws:
                        if v > wm.get(s, 0):
                            e.wait_ge(sems[s], v)
                            wm[s] = v
                    ins = fn(e)
                    if inc is not None:
                        ins.then_inc(sems[inc],
                                     16 if inc[0] == "d" else 1)

            getattr(block, eng_name)(body)


class Bank:

    def __init__(self, ap):
        self.ap = ap
        self.w_ev = None
        self.r_evs = []


def build_program(dbg=None, qhack=False, bstop=None):
    nc = bass.Bass("TRN2", target_bir_lowering=False)

    xo = nc.dram_tensor("xo", [TQ, H], F32, kind="ExternalInput").ap()
    xr = nc.dram_tensor("xr", [TQ, H], F32, kind="ExternalInput").ap()
    wq = nc.dram_tensor("wq", [H, H], F32, kind="ExternalInput").ap()
    wk = nc.dram_tensor("wk", [H, H], F32, kind="ExternalInput").ap()
    wv = nc.dram_tensor("wv", [H, H], F32, kind="ExternalInput").ap()
    wo = nc.dram_tensor("wo", [H, H], F32, kind="ExternalInput").ap()
    w1 = nc.dram_tensor("w1", [H, FF], F32, kind="ExternalInput").ap()
    w2 = nc.dram_tensor("w2", [FF, H], F32, kind="ExternalInput").ap()
    b1r = nc.dram_tensor("b1r", [128, 32], F32, kind="ExternalInput").ap()
    b2t = nc.dram_tensor("b2t", [128, H], F32, kind="ExternalInput").ap()
    rwr = nc.dram_tensor("rwr", [128, 64], F32, kind="ExternalInput").ap()
    idb = nc.dram_tensor("idb", [128, 128], BF16, kind="ExternalInput").ap()
    y = nc.dram_tensor("y", [TQ, H], F32, kind="ExternalOutput").ap()
    qs = nc.dram_tensor("qs", [NQG, 128, 8 * 512], BF16, kind="Internal").ap()
    x2s = nc.dram_tensor("x2s", [NQT, 128, H], F32, kind="Internal").ap()
    if dbg in ("A", "B"):
        dst_ = nc.dram_tensor("dst_", [128, 16], F32, kind="ExternalOutput").ap()
        dkf = nc.dram_tensor("dkf", [128, HC * TKV], BF16, kind="ExternalOutput").ap()
        dvt = nc.dram_tensor("dvt", [128, NKT * H], BF16, kind="ExternalOutput").ap()
        dqs = nc.dram_tensor("dqs", [NQG, 128, 8 * 512], BF16, kind="ExternalOutput").ap()
        dx2 = nc.dram_tensor("dx2", [NQT, 128, H], F32, kind="ExternalOutput").ap()

    E = Em()

    es = contextlib.ExitStack()
    S = lambda n, sh, dt: es.enter_context(nc.sbuf_tensor(n, sh, dt))
    PS = lambda n, sh: es.enter_context(nc.psum_tensor(n, sh, F32))

    ident = S("ident", [128, 128], BF16)
    eps = S("eps", [128, 1], F32)
    sm2 = S("sm2", [128, 2], F32)
    sq2 = S("sq2", [128, 2], F32)
    m2 = S("m2", [128, 2], F32)
    q2 = S("q2", [128, 2], F32)
    v2 = S("v2", [128, 2], F32)
    rstd = S("rstd", [128, 2], F32)
    nmb = S("nmb", [128, 2], F32)
    b1f = S("b1f", [128, 32], F32)
    rwb = S("rwb", [128, 64], BF16)
    sm4 = S("sm4", [128, 8], F32)
    sq4 = S("sq4", [128, 8], F32)
    m4 = S("m4", [128, 8], F32)
    q4 = S("q4", [128, 8], F32)
    v4 = S("v4", [128, 8], F32)
    rs4 = S("rs4", [128, 8], F32)
    nm4 = S("nm4", [128, 8], F32)
    dbgs = S("dbgs", [128, 16], F32)

    kv_es = contextlib.ExitStack()
    kf = kv_es.enter_context(nc.sbuf_tensor("kf", [128, HC * TKV], BF16))
    vt = kv_es.enter_context(nc.sbuf_tensor("vt", [128, NKT * H], BF16))

    pAll = PS("pAll", [128, 4096])
    BK = [Bank(pAll[:, i * 512:(i + 1) * 512]) for i in range(8)]

    def pair_ap(i):
        return pAll[:, i * 512:(i + 2) * 512]

    def mm_group(bank, mms, waits=()):
        n = len(mms)
        first_waits = list(waits) + list(bank.r_evs)
        ret = None
        for i, (o, l, r) in enumerate(mms):
            st = (i == 0)
            sp = (i == n - 1)

            def fn(e, o=o, l=l, r=r, st=st, sp=sp):
                return e.matmul(o(), l(), r(), start=st, stop=sp)

            w = first_waits if i == 0 else ()
            inc = "tS" if i == n - 1 else None
            ret = E.op("tensor", fn, waits=w, inc=inc) or ret
        bank.w_ev = ret
        bank.r_evs = []
        return ret

    def mm_group_pair(bankL, bankR, mmsL, mmsR, waits=()):
        n = len(mmsL)
        first_waits = list(waits) + list(bankL.r_evs) + list(bankR.r_evs)
        used_first = False
        ret = None
        for i in range(n):
            for which in (0, 1):
                mm = (mmsL, mmsR)[which][i]
                if mm is None:
                    continue
                o, l, r = mm
                st = (i == 0)
                sp = (i == n - 1)
                last = sp and (which == 1 or mmsR[i] is None)

                def fn(e, o=o, l=l, r=r, st=st, sp=sp):
                    return e.matmul(o(), l(), r(), start=st, stop=sp)

                w = () if used_first else first_waits
                inc = "tS" if last else None
                ev = E.op("tensor", fn, waits=w, inc=inc)
                used_first = True
                if ev:
                    ret = ev
        bankL.w_ev = ret
        bankR.w_ev = ret
        bankL.r_evs = []
        bankR.r_evs = []
        return ret

    def read_bank(eng, banks, fn, waits=(), inc=None):
        ws = list(waits)
        for b in banks:
            if b.w_ev is not None:
                ws.append(b.w_ev)
        ev = E.op(eng, fn, waits=ws, inc=inc)
        for b in banks:
            b.r_evs.append(ev)
        return ev

    def transpose_batch(bank, src_fn, js, waits=(), mov=None):
        n = len(js)
        first_waits = list(waits) + list(bank.r_evs)
        ret = None
        for idx, j in enumerate(js):
            def fn(e, idx=idx, j=j):
                m = ident[:, :] if mov is None else mov
                return e.matmul(bank.ap[:, idx * 128:(idx + 1) * 128],
                                src_fn(j), m, start=True, stop=True)
            w = first_waits if idx == 0 else ()
            inc = "tS" if idx == n - 1 else None
            ret = E.op("tensor", fn, waits=w, inc=inc) or ret
        bank.w_ev = ret
        bank.r_evs = []
        return ret

    ln_war = {0: None, 1: None}

    def ln_stats(i, src_fn, sq_fn, stat_waits_v=(), stat_waits_s=(),
                 src_banks=(), sq_cell=None, red_eng="vector"):
        p = i % 2
        sm = sm2[:, p:p + 1]
        sq = sq2[:, p:p + 1]
        m = m2[:, p:p + 1]
        q = q2[:, p:p + 1]
        v = v2[:, p:p + 1]
        rs = rstd[:, p:p + 1]
        nm = nmb[:, p:p + 1]
        fence = [ln_war[p]] if ln_war[p] else []
        if src_banks:
            ws = list(stat_waits_v) + fence
            for b_ in src_banks:
                if b_.w_ev is not None:
                    ws.append(b_.w_ev)
            ev_red = E.op("vector",
                          lambda e: e.reduce_sum(sm, src_fn(), axis=AX),
                          waits=ws, inc="vS")
            for b_ in src_banks:
                b_.r_evs.append(ev_red)
            ev_sq = read_bank("scalar", list(src_banks),
                              lambda e: e.activation(sq_fn(), src_fn(),
                                                     AF.Square, accum_out=sq),
                              waits=list(stat_waits_s) + fence, inc="sS")
        else:
            ev_red = E.op(red_eng,
                          lambda e: e.reduce_sum(sm, src_fn(), axis=AX),
                          waits=list(stat_waits_v) + fence,
                          inc="gS" if red_eng == "gpsimd" else "vS")
            sqw = list(stat_waits_s) + fence
            if sq_cell and sq_cell[0]:
                sqw.append(sq_cell[0])
            ev_sq = E.op("scalar",
                         lambda e: e.activation(sq_fn(), src_fn(), AF.Square,
                                                accum_out=sq),
                         waits=sqw, inc="sS")
            if sq_cell is not None:
                sq_cell[0] = ev_sq
        ev = E.op("scalar", lambda e: e.mul(m, sm, 1.0 / 1024),
                  waits=[ev_red, ev_sq], inc="sS")
        ev = E.op("scalar", lambda e: e.activation(v, m, AF.Square),
                  waits=[ev], inc="sS")
        ev = E.op("scalar", lambda e: e.mul(v, v, -1.0), waits=[ev], inc="sS")
        ev = E.op("scalar", lambda e: e.activation(q, sq, AF.Identity,
                                                   scale=1.0 / 1024, bias=v),
                  waits=[ev], inc="sS")
        ev = E.op("scalar", lambda e: e.activation(rs, q, AF.Ln,
                                                   bias=eps[:, 0:1]),
                  waits=[ev], inc="sS")
        ev = E.op("scalar", lambda e: e.activation(rs, rs, AF.Exp, scale=-0.5),
                  waits=[ev], inc="sS")
        ev = E.op("scalar", lambda e: e.mul(nm, m, -1.0), waits=[ev], inc="sS")
        ev_nm = E.op("scalar", lambda e: e.mul(nm, nm, rs), waits=[ev],
                     inc="sS")
        return ev_nm, rs, nm

    def ln_register_apply(i, ev_apply):
        ln_war[i % 2] = ev_apply

    def ln_chain(i, src_fn, out_fn, sq_fn, extra_stat_waits=(),
                 extra_apply_waits=()):
        ev_nm, rs, nm = ln_stats(i, src_fn, sq_fn,
                                 stat_waits_v=extra_stat_waits,
                                 stat_waits_s=extra_stat_waits)
        ev_apply = E.op("scalar",
                        lambda e: e.activation(out_fn(), src_fn(), AF.Identity,
                                               bias=nm, scale=rs),
                        waits=[ev_nm] + list(extra_apply_waits), inc="sS")
        ln_register_apply(i, ev_apply)
        return ev_apply

    def ln_psum_epilogue(qt, bankL, bankR, full_ap, out_fn, sq_fn,
                         extra_apply_waits=()):
        p = qt % 2
        fence = [ln_war[p]] if ln_war[p] else []
        ev_cp = read_bank("scalar", [bankL, bankR],
                          lambda e: e.copy(out_fn(), full_ap),
                          waits=fence + list(extra_apply_waits), inc="sS")
        ev_nm, rs, nm = ln_stats(qt, out_fn, sq_fn,
                                 stat_waits_v=[ev_cp], stat_waits_s=[ev_cp])
        ev_ao = E.op("scalar",
                     lambda e: e.activation(out_fn(), out_fn(), AF.Identity,
                                            bias=nm, scale=rs),
                     waits=[ev_nm], inc="sS")
        ln_register_apply(qt, ev_ao)
        return ev_ao

    E.op("vector", lambda e: e.memset(eps[:, :], 1e-5))
    ev_ident = E.op("sync", lambda e: e.dma_start(ident[:, :], idb[:, :]),
                    inc="dSy")
    ev_b1 = E.op("sync", lambda e: e.dma_start(b1f[:, :], b1r[:, :]),
                 inc="dSy")

    a_es = contextlib.ExitStack()
    SA = lambda n, sh, dt: a_es.enter_context(nc.sbuf_tensor(n, sh, dt))
    wqb = SA("wqb", [128, HC * H], BF16)
    wkb = SA("wkb", [128, HC * H], BF16)
    wvb = SA("wvb", [128, HC * H], BF16)
    xgb = SA("xgb", [128, 4 * H], BF16)
    xng = SA("xng", [128, 4 * H], BF16)
    xfg = SA("xfg", [128, HC * 512], BF16)
    qstage = SA("qstage", [128, 1024], BF16)
    sqa = SA("sqa", [128, 1024], BF16)

    def wld(dst, src):
        return E.op("gpsimd", lambda e, d=dst, s=src: e.dma_start(
            d[:, :].rearrange("p (c o) -> p c o", c=HC),
            s.rearrange("(c p) o -> p c o", p=128)), inc="dGp")

    ev_wq = wld(wqb, wq)
    ev_wk = wld(wkb, wk)
    ev_wv = wld(wvb, wv)
    ev_rw = E.op("gpsimd", lambda e: e.dma_start(rwb[:, :], rwr[:, :]),
                 inc="dGp")

    tbanks = [BK[3], BK[4]]
    pbanks = [BK[0], BK[1], BK[2], BK[7]]
    ev_apply = {}
    ev_evacT = {}
    ev_qs_store = {}
    qs_prev = [None]
    ev_qev = {}
    ev_vevac = {}
    ev_transp_last = {}
    last_xg_readers = {}

    ln4_war = {0: (None, None), 1: (None, None)}
    last_xgb_readers = {}
    ev_transp_g = {}
    sqa_prev = [None]
    ev_proj_g = {}

    ln_dma = {}

    def emit_ln_loads(g):
        src_x = xo if g < NQG else xr
        base = g * 512 if g < NQG else (g - NQG) * 512
        evs = []
        for t in range(4):
            row = base + t * 128
            dst = xgb[:, t * H:(t + 1) * H]
            w = list(last_xgb_readers.get(g - 1, []))
            evs.append(E.op("gpsimd", lambda e, r=row, s=src_x, d=dst:
                            e.dma_start(d, s[r:r + 128, :]), waits=w,
                            inc="dGp"))
        ln_dma[g] = evs

    def emit_ln_stats(g):
        p4 = (g % 2) * 4
        fence_s, fence_v = ln4_war[g % 2]
        ev_red = None
        ev_sq = sqa_prev[0]
        for t in range(4):
            ev_dma = ln_dma[g][t]
            dst = xgb[:, t * H:(t + 1) * H]
            wv = [ev_dma] + ([fence_v] if fence_v and t == 0 else [])
            if ev_sq:
                wv.append(ev_sq)
            ev_red = E.op("scalar", lambda e, t=t, d=dst:
                          e.activation(sqa[:, :], d, AF.Identity,
                                       accum_out=sm4[:, p4 + t:p4 + t + 1]),
                          waits=wv, inc="sS")
            ev_sq = ev_red
            ws = [ev_dma] + ([fence_s] if fence_s and t == 0 else [])
            if ev_sq:
                ws.append(ev_sq)
            ev_sq = E.op("scalar", lambda e, t=t, d=dst:
                         e.activation(sqa[:, :], d, AF.Square,
                                      accum_out=sq4[:, p4 + t:p4 + t + 1]),
                         waits=ws, inc="sS")
        sqa_prev[0] = ev_sq
        sl = slice(p4, p4 + 4)
        ev = E.op("vector", lambda e: e.tensor_scalar(
            m4[:, sl], sm4[:, sl], 1.0 / 1024, None, ALU.mult),
            waits=[ev_red], inc="vS")
        ev = E.op("vector", lambda e: e.tensor_tensor(
            v4[:, sl], m4[:, sl], m4[:, sl], ALU.mult), waits=[ev], inc="vS")
        ev = E.op("vector", lambda e: e.tensor_scalar(
            q4[:, sl], sq4[:, sl], 1.0 / 1024, None, ALU.mult),
            waits=[ev, ev_sq], inc="vS")
        ev = E.op("vector", lambda e: e.tensor_tensor(
            v4[:, sl], q4[:, sl], v4[:, sl], ALU.subtract),
            waits=[ev], inc="vS")
        ev = E.op("scalar", lambda e: e.activation(
            rs4[:, sl], v4[:, sl], AF.Ln, bias=eps[:, 0:1]),
            waits=[ev], inc="sS")
        ev = E.op("scalar", lambda e: e.activation(
            rs4[:, sl], rs4[:, sl], AF.Exp, scale=-0.5), waits=[ev], inc="sS")
        ev_nm = E.op("vector", lambda e: e.scalar_tensor_tensor(
            nm4[:, sl], m4[:, sl], -1.0, rs4[:, sl], ALU.mult, ALU.mult),
            waits=[ev], inc="vS")
        ev_ap = None
        for t in range(4):
            aw = [ev_nm]
            if ev_ap:
                aw.append(ev_ap)
            if g - 1 in ev_transp_g and t == 0:
                aw.append(ev_transp_g[g - 1])
            ev_ap = E.op("scalar", lambda e, t=t: e.activation(
                xng[:, t * H:(t + 1) * H], xgb[:, t * H:(t + 1) * H],
                AF.Identity, bias=nm4[:, p4 + t:p4 + t + 1],
                scale=rs4[:, p4 + t:p4 + t + 1]),
                waits=aw, inc="sS")
            ev_apply[4 * g + t] = ev_ap
        ln4_war[g % 2] = (ev_ap, ev_nm)
        last_xgb_readers[g] = [ev_ap, ev_red]

    def emit_transp_group(g):
        for t in range(4):
            i = 4 * g + t
            xnga = xng[:, t * H:(t + 1) * H]
            for b in range(2):
                bank = tbanks[(i * 2 + b) % 2]
                wtr = [ev_apply[i], ev_ident]
                if t == 0 and b == 0 and g - 1 in ev_proj_g:
                    wtr.append(ev_proj_g[g - 1])
                ev_tr = transpose_batch(
                    bank, lambda j, a=xnga: a[:, j * 128:(j + 1) * 128],
                    list(range(4 * b, 4 * b + 4)),
                    waits=wtr)
                if t == 3 and b == 1:
                    ev_transp_g[g] = ev_tr
                dst = xfg[:, :].rearrange("p (c t) -> p c t", t=512)[
                    :, 4 * b:4 * b + 4, t * 128:(t + 1) * 128]
                srcb = bank.ap.rearrange("p (c t) -> p c t", c=4)
                if b == 0:
                    ev = read_bank("vector", [bank],
                                   lambda e, d=dst, s=srcb: e.tensor_copy(d, s),
                                   inc="vS")
                else:
                    ev = read_bank("scalar", [bank],
                                   lambda e, d=dst, s=srcb: e.copy(d, s),
                                   inc="sS")
                ev_evacT[(i, b)] = ev

    def proj_group(g, kind):
        wb = wqb if kind == "q" else wkb
        wev = ev_wq if kind == "q" else ev_wk
        xfg_ready = [ev_evacT[(g * 4 + t, b)] for t in range(4) for b in range(2)]
        for oc in range(HC):
            bank = pbanks[oc % 4]
            mms = []
            for hc in range(HC):
                mms.append((
                    (lambda bk=bank: bk.ap),
                    (lambda hc=hc, oc=oc, w=wb: w[:, hc * H + oc * 128: hc * H + oc * 128 + 128]),
                    (lambda hc=hc: xfg[:, hc * 512:(hc + 1) * 512]),
                ))
            mm_group(bank, mms, waits=xfg_ready + [wev])
            if kind == "k":
                dst = kf[:, oc * TKV + g * 512: oc * TKV + (g + 1) * 512]
                read_bank("vector", [bank],
                          lambda e, d=dst, s=bank.ap: e.tensor_copy(d, s),
                          inc="vS")
            else:
                half = oc % 2
                dst = qstage[:, half * 512:(half + 1) * 512]
                w2_ = [qs_prev[0]] if qs_prev[0] else []
                ev = read_bank("vector", [bank],
                               lambda e, d=dst, s=bank.ap: e.tensor_copy(d, s),
                               waits=w2_, inc="vS")
                ev_qev[half] = ev
                if half == 1:
                    ocp = oc // 2
                    dstq = qs[g][:, ocp * 1024:(ocp + 1) * 1024]
                    evs = E.op("gpsimd",
                               lambda e, d=dstq: e.dma_start(d, qstage[:, :]),
                               waits=[ev_qev[0], ev_qev[1]], inc="dGp")
                    ev_qs_store[(g, oc)] = evs
                    ev_qs_store[(g, oc - 1)] = evs
                    qs_prev[0] = evs

    def proj_v(g):
        xfg_ready = [ev_evacT[(g * 4 + t, b)] for t in range(4) for b in range(2)]
        for ts in range(4):
            tile = g * 4 + ts
            bkL, bkR = BK[5], BK[6]
            full = pair_ap(5)
            mmsL, mmsR = [], []
            for hc in range(HC):
                lhs = (lambda hc=hc, ts=ts: xfg[:, hc * 512 + ts * 128: hc * 512 + ts * 128 + 128])
                mmsL.append(((lambda b=bkL: b.ap), lhs,
                             (lambda hc=hc: wvb[:, hc * H: hc * H + 512])))
                mmsR.append(((lambda b=bkR: b.ap), lhs,
                             (lambda hc=hc: wvb[:, hc * H + 512: hc * H + 1024])))
            mm_group_pair(bkL, bkR, mmsL, mmsR, waits=xfg_ready + [ev_wv])
            dstL = vt[:, tile * H: tile * H + 512]
            dstR = vt[:, tile * H + 512:(tile + 1) * H]
            evL = read_bank("vector", [bkL],
                            lambda e, d=dstL, s=bkL.ap: e.tensor_copy(d, s),
                            inc="vS")
            evR = read_bank("scalar", [bkR],
                            lambda e, d=dstR, s=bkR.ap: e.copy(d, s),
                            inc="sS")
            ev_vevac[tile] = (evL, evR)

    def emit_proj(g):
        if g < NQG:
            proj_group(g, "q")
        proj_group(g, "k")
        proj_v(g)

    for g in range(NKG):
        emit_ln_loads(g)
        if g >= 1:
            if g - 1 < NQG:
                proj_group(g - 1, "q")
            emit_ln_stats(g)
            proj_group(g - 1, "k")
            proj_v(g - 1)
            ev_proj_g[g - 1] = E.ev("tS")
        else:
            emit_ln_stats(g)
        emit_transp_group(g)
    emit_proj(NKG - 1)

    ev_A_t = E.ev("tS")
    ev_A_v = E.ev("vS")
    ev_A_s = E.ev("sS")
    ev_A_end = [ev_A_t, ev_A_v, ev_A_s, ev_qs_store[(NQG - 1, 7)]]

    if dbg == "A":
        E.op("gpsimd", lambda e: e.dma_start(dst_[:, :], dbgs[:, :]),
             waits=[ev_A_v], inc="dGp")
        E.op("gpsimd", lambda e: e.dma_start(dkf[:, :], kf[:, :]), inc="dGp")
        E.op("gpsimd", lambda e: e.dma_start(dvt[:, :], vt[:, :]), inc="dGp")
        for g in range(NQG):
            E.op("gpsimd", lambda e, g=g: e.dma_start(dqs[g], qs[g]), inc="dGp")
        E.op("gpsimd", lambda e: e.nop(),
             waits=[(f"dGp{i}", E.counts[f"dGp{i}"]) for i in range(NDMA)])
        E.op("sync", lambda e: e.nop(),
             waits=[(f"dSy{i}", E.counts[f"dSy{i}"]) for i in range(NDMA)])
        semctx = contextlib.ExitStack()
        sems = {s: semctx.enter_context(nc.semaphore(s)) for s in SEM_NAMES}
        with nc.Block("init") as b0:
            def _clr(e):
                for s_ in sems.values():
                    e.sem_clear(s_)
            b0.gpsimd(_clr)
        with nc.Block() as block:
            E.replay(block, sems)
        semctx.close()
        a_es.close()
        kv_es.close()
        es.close()
        return nc

    a_es.close()
    b_es = contextlib.ExitStack()
    SB = lambda n, sh, dt: b_es.enter_context(nc.sbuf_tensor(n, sh, dt))
    wob = SB("wob", [128, HC * H], BF16)
    qtile = SB("qtile", [128, 2 * 1024], BF16)
    pbuf = SB("pbuf", [128, 2 * 4096], BF16)
    ptb = SB("ptb", [128, 4096], BF16)
    ctm = SB("ctm", [128, 1024], BF16)
    cfm = SB("cfm", [128, 1024], BF16)
    aot = SB("aot", [128, 2 * 1024], F32)
    sqb = SB("sqb", [128, 2 * 1024], BF16)
    xrt = SB("xrt", [128, 2 * 1024], F32)
    x2t = SB("x2t", [128, 2 * 1024], F32)
    diagb = SB("diagb", [128, 1024], BF16)
    m8 = SB("m8", [128, 16], F32)
    nb8 = SB("nb8", [128, 16], F32)
    l8 = SB("l8", [128, 16], F32)
    d8 = SB("d8", [128, 16], F32)
    e8 = SB("e8", [128, 16], F32)
    cf8 = SB("cf8", [128, 16], F32)
    lw8 = SB("lw8", [128, 16], F32)
    mg2 = SB("mg2", [128, 2], F32)
    lg2 = SB("lg2", [128, 2], F32)
    li2 = SB("li2", [128, 2], F32)

    E.op("vector", lambda e: e.nop(), waits=ev_A_end)
    E.op("scalar", lambda e: e.nop(), waits=ev_A_end)
    ev_wo = E.op("gpsimd", lambda e: e.dma_start(
        wob[:, :].rearrange("p (c o) -> p c o", c=HC),
        wo.rearrange("(c p) o -> p c o", p=128)),
        waits=ev_A_end, inc="dGp")

    sc_banks = [BK[0], BK[1], BK[2], BK[7]]
    tr_banks = [BK[3], BK[4]]
    ev_qdma = {}
    ev_xrt = {}
    ev_exp = {}
    ev_rescale = {}
    ev_diag = {}
    ev_transp_last = {}
    ev_transpPT = {}
    ev_ctm = {}
    ev_ctxT = {}
    ev_ao_mm = {}
    ev_aocp = {}
    ev_epi_ln = {}
    ev_evacCT = {}
    ev_add = {}
    ev_x2store = {}
    last_scores_stop = {}
    kf_free = {}
    vt_free = [None]
    w1b_ref = []
    ev_w1 = []
    gsub = [0]

    def emit_w1_chunk(ks):
        def fn(e, ks=ks):
            dstw = w1b_ref[0][:, :].rearrange(
                "p (c f) -> p c f", f=FF)[:, :, ks * 512:(ks + 1) * 512]
            srcw = w1[:, ks * 512:(ks + 1) * 512].rearrange(
                "(c p) f -> p c f", p=128)
            return e.dma_start(dstw, srcw)
        ev_w1.append(E.op("gpsimd", fn, waits=[kf_free[ks]], inc="dGp"))

    ev_kready = {g: list(ev_vevac[g * 4 + 3]) for g in range(NKG)}

    def emit_qdma(qt):
        g, k = qt // 4, qt % 4
        p = qt % 2
        w = [ev_qs_store[(g, 7)]]
        if qt == 0:
            w += ev_A_end
        if qt - 2 >= 0:
            w.append(last_scores_stop[qt - 2])
        dst = qtile[:, p * 1024:(p + 1) * 1024].rearrange(
            "p (c t) -> p c t", t=128)
        src = qs[g].rearrange("p (c t) -> p c t", t=512)[:, :, k * 128:(k + 1) * 128]
        if qhack:
            dflat = qtile[:, p * 1024:(p + 1) * 1024]
            sflat = qs[g][:, 0:1024]
            ev1 = E.op("sync", lambda e, d=dflat, s=sflat: e.dma_start(d, s),
                       waits=w, inc="dSy")
            ev_qdma[qt] = [ev1]
        else:
            ev1 = E.op("sync", lambda e, d=dst[:, 0:4], s=src[:, 0:4]:
                       e.dma_start(d, s), waits=w, inc="dSy")
            ev2 = E.op("sync", lambda e, d=dst[:, 4:8], s=src[:, 4:8]:
                       e.dma_start(d, s), inc="dSy")
            ev_qdma[qt] = [ev1, ev2]

    def emit_xrt(qt):
        p = qt % 2
        w = []
        if qt <= 1:
            w = list(ev_A_end)
        if qt - 2 >= 0 and qt - 2 in ev_add:
            w.append(ev_add[qt - 2])
        dst = xrt[:, p * 1024:(p + 1) * 1024]
        ev_xrt[qt] = E.op("gpsimd", lambda e, d=dst, r=qt * 128:
                          e.dma_start(d, xo[r:r + 128, :]), waits=w, inc="dGp")

    ev_rescale_l = {}
    ev_add_l = {}
    ev_aoev = {}

    def emit_scores(qt, ks0=0, ks1=8):
        p = qt % 2
        qa = qtile[:, p * 1024:(p + 1) * 1024]
        fence_v = list(filter(None, [ev_exp.get((qt - 2, 7)),
                                     ev_rescale.get(qt - 1)])) if ks0 == 0 else []
        fence_s = list(filter(None, [ev_rescale.get(qt - 2),
                                     ev_transp_last.get(qt - 2)])) if ks0 == 0 else []
        for ks in range(ks0, ks1):
            bank = sc_banks[gsub[0] % 4]
            gsub[0] += 1
            mms = []
            for hc in range(HC):
                mms.append((
                    (lambda b=bank: b.ap),
                    (lambda hc=hc, qa=qa: qa[:, hc * 128:(hc + 1) * 128]),
                    (lambda hc=hc, ks=ks: kf[:, hc * TKV + ks * 512: hc * TKV + (ks + 1) * 512]),
                ))
            ev_stop = mm_group(bank, mms, waits=ev_qdma[qt] + ev_kready[ks])
            if qt == NQT - 1:
                kf_free[ks] = ev_stop
            if ks == 7:
                last_scores_stop[qt] = ev_stop
            mxa = m8[:, p * 8 + ks: p * 8 + ks + 1]
            nba = nb8[:, p * 8 + ks: p * 8 + ks + 1]
            ws = [bank.w_ev] if bank.w_ev is not None else []
            ev_red = E.op("vector", lambda e, d=mxa, s=bank.ap:
                          e.reduce_max(d, s, axis=AX),
                          waits=ws + fence_v, inc="vS")
            fence_v = []
            bank.r_evs.append(ev_red)
            ev_nb = E.op("vector", lambda e, d=nba, s=mxa:
                         e.tensor_scalar(d, s, -0.03125, None, ALU.mult),
                         waits=[ev_red], inc="vS")
            dstp = pbuf[:, p * 4096 + ks * 512: p * 4096 + (ks + 1) * 512]
            acc = l8[:, p * 8 + ks: p * 8 + ks + 1]
            ev = read_bank("scalar", [bank],
                           lambda e, d=dstp, s=bank.ap, nb=nba, a=acc:
                           e.activation(d, s, AF.Exp, bias=nb, scale=0.03125,
                                        accum_out=a),
                           waits=[ev_nb] + fence_s, inc="sS")
            fence_s = []
            ev_exp[(qt, ks)] = ev

    def emit_softmax_tail(qt):
        if bstop == "scores":
            return
        p = qt % 2
        m8a = m8[:, p * 8:(p + 1) * 8]
        l8a = l8[:, p * 8:(p + 1) * 8]
        d8a = d8[:, p * 8:(p + 1) * 8]
        e8a = e8[:, p * 8:(p + 1) * 8]
        cfa = cf8[:, p * 8:(p + 1) * 8]
        lwa = lw8[:, p * 8:(p + 1) * 8]
        mga = mg2[:, p:p + 1]
        lga = lg2[:, p:p + 1]
        lia = li2[:, p:p + 1]
        ev = E.op("vector", lambda e: e.reduce_max(mga, m8a, axis=AX),
                  waits=[ev_exp[(qt, 7)]], inc="vS")
        ev_d = E.op("vector", lambda e: e.tensor_scalar(
            d8a, m8a, mga, None, ALU.subtract), waits=[ev], inc="vS")
        ev_e8 = E.op("scalar", lambda e: e.activation(
            e8a, d8a, AF.Exp, scale=0.03125), waits=[ev_d], inc="sS")
        ev = E.op("vector", lambda e: e.tensor_tensor(lwa, l8a, e8a, ALU.mult),
                  waits=[ev_e8], inc="vS")
        ev = E.op("vector", lambda e: e.reduce_sum(lga, lwa, axis=AX),
                  waits=[ev], inc="vS")
        ev = E.op("vector", lambda e: e.reciprocal(lia, lga), waits=[ev],
                  inc="vS")
        ev = E.op("vector", lambda e: e.tensor_scalar(cfa, e8a, lia, None,
                                                      ALU.mult),
                  waits=[ev], inc="vS")
        ev_rescale[qt] = ev
        wd = [ev]
        if qt - 1 in ev_transp_last:
            wd.append(ev_transp_last[qt - 1])
        for ks in range(8):
            da = diagb[:, ks * 128:(ks + 1) * 128]
            cfs = cfa[:, ks:ks + 1]
            ev_diag[(qt, ks)] = E.op("scalar", lambda e, d=da, s=cfs:
                                     e.activation(d, ident[:, :], AF.Identity,
                                                  scale=s),
                                     waits=wd, inc="sS")
            wd = []

    def emit_tail_transp(qt):
        p = qt % 2
        src_p = pbuf[:, p * 4096:(p + 1) * 4096]
        fence_pt = [ev_transpPT[(qt - 1, 7)]] if (qt - 1, 7) in ev_transpPT else []
        for b in range(8):
            bank = tr_banks[b % 2]
            ev_tr = transpose_batch(bank,
                            lambda j, s=src_p: s[:, j * 128:(j + 1) * 128],
                            list(range(4 * b, 4 * b + 4)),
                            waits=[ev_diag[(qt, b)]],
                            mov=diagb[:, b * 128:(b + 1) * 128])
            if b == 7:
                ev_transp_last[qt] = ev_tr
            dst = ptb[:, b * 512:(b + 1) * 512]
            if b % 2 == 0:
                ev = read_bank("vector", [bank],
                               lambda e, d=dst, s=bank.ap: e.tensor_copy(d, s),
                               waits=fence_pt, inc="vS")
            else:
                ev = read_bank("scalar", [bank],
                               lambda e, d=dst, s=bank.ap: e.copy(d, s),
                               waits=fence_pt, inc="sS")
            fence_pt = []
            ev_transpPT[(qt, b)] = ev

    def emit_tail_ctx(qt):
        p = qt % 2
        mmsL, mmsR = [], []
        for kb in range(NKT):
            lhs = (lambda kb=kb: ptb[:, kb * 128:(kb + 1) * 128])
            mmsL.append(((lambda: BK[5].ap), lhs,
                         (lambda kb=kb: vt[:, kb * H: kb * H + 512])))
            mmsR.append(((lambda: BK[6].ap), lhs,
                         (lambda kb=kb: vt[:, kb * H + 512: kb * H + 1024])))
        wctx = ([ev_transpPT[(qt, b)] for b in range(8)]
                + list(ev_vevac[NKT - 1]))
        ev_ctx_stop = mm_group_pair(BK[5], BK[6], mmsL, mmsR, waits=wctx)
        if qt == NQT - 1:
            vt_free[0] = ev_ctx_stop
        ctma = ctm[:, :]
        wctm = [ev_ctxT[qt - 1]] if qt - 1 in ev_ctxT else []
        ev_ctm[qt] = read_bank("scalar", [BK[5], BK[6]],
                               lambda e, d=ctma: e.copy(d, pair_ap(5)),
                               waits=wctm, inc="sS")

    def emit_tail_out(qt):
        p = qt % 2
        ctma = ctm[:, :]
        cfma = cfm[:, :]
        for b in range(2):
            bank = tr_banks[b % 2]
            ev_tr = transpose_batch(bank,
                            lambda j, s=ctma: s[:, j * 128:(j + 1) * 128],
                            list(range(4 * b, 4 * b + 4)),
                            waits=[ev_ctm[qt]])
            if b == 1:
                ev_ctxT[qt] = ev_tr
            wcf = [ev_ao_mm[qt - 1]] if (b == 0 and qt - 1 in ev_ao_mm) else []
            dst = cfma[:, b * 512:(b + 1) * 512]
            ev = read_bank("vector", [bank],
                           lambda e, d=dst, s=bank.ap: e.tensor_copy(d, s),
                           waits=wcf, inc="vS")
            ev_evacCT[(qt, b)] = ev
        mmsL, mmsR = [], []
        for hc in range(HC):
            lhs = (lambda hc=hc, c=cfma: c[:, hc * 128:(hc + 1) * 128])
            mmsL.append(((lambda: BK[3].ap), lhs,
                         (lambda hc=hc: wob[:, hc * H: hc * H + 512])))
            mmsR.append(((lambda: BK[4].ap), lhs,
                         (lambda hc=hc: wob[:, hc * H + 512: hc * H + 1024])))
        ev_mm = mm_group_pair(BK[3], BK[4], mmsL, mmsR,
                              waits=[ev_evacCT[(qt, 0)], ev_evacCT[(qt, 1)],
                                     ev_wo])
        ev_ao_mm[qt] = ev_mm
        aota = aot[:, p * 1024:(p + 1) * 1024]
        wcp = [ev_add[qt - 2]] if qt - 2 in ev_add else []
        ev_aocp[qt] = read_bank("scalar", [BK[3], BK[4]],
                                lambda e, d=aota: e.copy(d, pair_ap(3)),
                                waits=wcp, inc="sS")

    def emit_epi_ln(qt):
        p = qt % 2
        aota = aot[:, p * 1024:(p + 1) * 1024]
        ev_nm, rs_ap, nm_ap = ln_stats(qt, lambda a=aota: a,
                                       lambda: sqb[:, p * 1024:(p + 1) * 1024],
                                       stat_waits_v=[ev_aocp[qt]],
                                       stat_waits_s=[ev_aocp[qt]])
        ev_ln = E.op("scalar", lambda e, a=aota, rs_=rs_ap, nm_=nm_ap:
                     e.activation(a, a, AF.Identity, bias=nm_, scale=rs_),
                     waits=[ev_nm], inc="sS")
        ln_register_apply(qt, ev_ln)
        ev_epi_ln[qt] = ev_ln
        ev_aoev[qt] = ev_ln

    def emit_epi_add(qt):
        p = qt % 2
        aota = aot[:, p * 1024:(p + 1) * 1024]
        xra = xrt[:, p * 1024:(p + 1) * 1024]
        x2a = x2t[:, p * 1024:(p + 1) * 1024]
        wadd = [ev_epi_ln[qt], ev_xrt[qt]]
        if qt - 2 >= 0 and qt - 2 in ev_x2store:
            wadd.append(ev_x2store[qt - 2])
        ev_a = E.op("gpsimd", lambda e, d=x2a, a=aota, b=xra:
                    e.tensor_tensor(d, a, b, ALU.add), waits=wadd, inc="gS")
        ev_add[qt] = ev_a
        ev_add_l[qt] = ev_a
        ev_x2store[qt] = E.op("sync", lambda e, s=x2a, t=qt:
                              e.dma_start(x2s[t], s), waits=[ev_a], inc="dSy")

    emit_qdma(0)
    emit_xrt(0)
    emit_xrt(1)
    emit_scores(0)
    emit_softmax_tail(0)
    for qt in range(NQT):
        nxt = qt + 1 < NQT
        if nxt:
            emit_qdma(qt + 1)
            emit_scores(qt + 1, 0, 5)
            if qt == NQT - 2 and dbg is None:
                for hc in range(5):
                    emit_w1_chunk(hc)
        if qt - 2 >= 0:
            emit_epi_ln(qt - 2)
        emit_tail_transp(qt)
        if nxt:
            emit_scores(qt + 1, 5, 7)
            if qt == NQT - 2 and dbg is None:
                emit_w1_chunk(5)
                emit_w1_chunk(6)
        emit_tail_ctx(qt)
        if nxt:
            emit_scores(qt + 1, 7, 8)
            if qt == NQT - 2 and dbg is None:
                emit_w1_chunk(7)
        if qt - 2 >= 0:
            emit_epi_add(qt - 2)
            emit_xrt(qt)
        if not nxt:
            emit_epi_ln(qt - 1)
        emit_tail_out(qt)
        if nxt:
            emit_softmax_tail(qt + 1)
    emit_epi_add(NQT - 2)
    emit_epi_ln(NQT - 1)
    emit_epi_add(NQT - 1)

    ev_last_scores = last_scores_stop[NQT - 1]
    ev_B_t = E.ev("tS")
    ev_B_v = E.ev("vS")
    ev_B_s = E.ev("sS")

    if dbg == "B":
        wlast = list(ev_x2store.values())
        for t in range(NQT):
            E.op("gpsimd", lambda e, t=t: e.dma_start(dx2[t], x2s[t]),
                 waits=wlast, inc="dGp")
        E.op("gpsimd", lambda e: e.dma_start(dkf[:, :], kf[:, :]), inc="dGp")
        E.op("gpsimd", lambda e: e.dma_start(dvt[:, :], vt[:, :]), inc="dGp")
        for g in range(NQG):
            E.op("gpsimd", lambda e, g=g: e.dma_start(dqs[g], qs[g]), inc="dGp")
        E.op("gpsimd", lambda e: e.nop(),
             waits=[(f"dGp{i}", E.counts[f"dGp{i}"]) for i in range(NDMA)])
        E.op("sync", lambda e: e.nop(),
             waits=[(f"dSy{i}", E.counts[f"dSy{i}"]) for i in range(NDMA)])
        semctx = contextlib.ExitStack()
        sems = {s: semctx.enter_context(nc.semaphore(s)) for s in SEM_NAMES}
        with nc.Block("init") as b0:
            def _clr(e):
                for s_ in sems.values():
                    e.sem_clear(s_)
            b0.gpsimd(_clr)
        with nc.Block() as block:
            E.replay(block, sems)
        semctx.close()
        b_es.close()
        kv_es.close()
        es.close()
        return nc

    b_es.close()
    kv_es.close()
    c_es = contextlib.ExitStack()
    SC = lambda n, sh, dt: c_es.enter_context(nc.sbuf_tensor(n, sh, dt))
    w1b = SC("w1b", [128, HC * FF], BF16)
    w2b = SC("w2b", [128, NFC * H], BF16)
    h1g = SC("h1g", [128, NFC * 512], BF16)
    x2lt = SC("x2lt", [128, 4 * H], BF16)
    x2f = SC("x2f", [128, HC * 512], BF16)
    x2rt = SC("x2rt", [128, 2 * H], F32)
    b2bc = SC("b2bc", [128, H], F32)
    eot = SC("eot", [128, H], F32)
    yt = SC("yt", [128, H], F32)
    sqc = SC("sqc", [128, 1024], BF16)
    rt8 = SC("rt8", [128, 32], F32)
    rds = SC("rds", [128, 512], BF16)
    rdn = SC("rdn", [128, 4 * 512], BF16)
    onesb = SC("onesb", [128, 128], BF16)
    idb4 = SC("idb4", [128, 512], BF16)
    prb8 = SC("prb8", [128, 8], F32)
    pn8 = SC("pn8", [128, 8], F32)
    t8 = SC("t8", [128, 8], F32)
    rsm = SC("rsm", [128, 4], F32)
    sc4 = SC("sc4", [128, 4], F32)

    w1b_ref.append(w1b)
    ev_w2 = []

    def emit_w2():
        for r in range(4):
            srcw = w2[r * 1024:(r + 1) * 1024, :].rearrange(
                "(c p) o -> p c o", p=128)
            dstw = w2b[:, r * 8192:(r + 1) * 8192].rearrange(
                "p (c o) -> p c o", c=8)
            ev_w2.append(E.op("gpsimd",
                              lambda e, d=dstw, s=srcw: e.dma_start(d, s),
                              waits=[vt_free[0]], inc="dGp"))

    ev_b2 = E.op("sync", lambda e: e.dma_start(b2bc[:, :], b2t[:, :]),
                 waits=[ev_B_t, ev_B_v, ev_B_s, ev_x2store[NQT - 1]], inc="dSy")

    CB0 = [ev_B_t, ev_B_v, ev_B_s, ev_x2store[NQT - 2], ev_x2store[NQT - 1]]
    ev_ones = E.op("vector", lambda e: e.memset(onesb[:, :], 1.0),
                   waits=CB0, inc="vS")
    ev_id4 = ev_ones
    for j in range(4):
        ev_id4 = E.op("scalar", lambda e, j=j: e.copy(
            idb4[:, j * 128:(j + 1) * 128], ident[:, :]),
            waits=[ev_id4, ev_ident], inc="sS")

    ffn_banks = [BK[0], BK[1], BK[2]]
    ctr_banks = [BK[3], BK[4]]
    rt_banks = [BK[7], BK[5]]
    ev_gelu = {}
    ev_yadd = {}
    ev_ystore = {}
    ev_x2f_evac = {}
    ev_sc = {}
    csq_prev = [None]
    x2lt_readers = {}
    ev_ctr_g = {}
    cx2f_free = [None]
    rt_state = {"prev_v": None, "prev_s": None, "prev_mx": None}

    ev_diag_c = {}
    tr_last_c = {}
    stats_evs = {}
    rt_slice_prev = {}

    def c_stats(gc):
        p4 = (gc % 2) * 4
        fence_s, fence_v = ln4_war[gc % 2]
        ev_red = None
        ev_sq = csq_prev[0]
        for t in range(4):
            qt = gc * 4 + t
            dst = x2lt[:, t * H:(t + 1) * H]
            w = list(x2lt_readers.get(gc - 1, []))
            if gc == 0 and t == 0:
                w += [ev_B_t, ev_B_v, ev_B_s]
            w.append(ev_x2store[qt])
            ev_dma = E.op("gpsimd", lambda e, d=dst, t_=qt:
                          e.dma_start(d, x2s[t_]), waits=w, inc="dGp")
            wv = [ev_dma] + ([fence_v] if fence_v and t == 0 else [])
            ev_red = E.op("vector", lambda e, t=t, d=dst:
                          e.reduce_sum(sm4[:, p4 + t:p4 + t + 1], d, axis=AX),
                          waits=wv, inc="vS")
            ws = [ev_dma]
            if gc == 0 and t == 0:
                ws.append(ev_x2store[NQT - 2])
            if ev_sq:
                ws.append(ev_sq)
            ev_sq = E.op("scalar", lambda e, t=t, d=dst:
                         e.activation(sqc[:, :], d, AF.Square,
                                      accum_out=sq4[:, p4 + t:p4 + t + 1]),
                         waits=ws, inc="sS")
        csq_prev[0] = ev_sq
        sl = slice(p4, p4 + 4)
        ev = E.op("vector", lambda e: e.tensor_scalar(
            m4[:, sl], sm4[:, sl], 1.0 / 1024, None, ALU.mult),
            waits=[ev_red], inc="vS")
        ev = E.op("vector", lambda e: e.tensor_tensor(
            v4[:, sl], m4[:, sl], m4[:, sl], ALU.mult), waits=[ev], inc="vS")
        ev = E.op("vector", lambda e: e.tensor_scalar(
            q4[:, sl], sq4[:, sl], 1.0 / 1024, None, ALU.mult),
            waits=[ev, ev_sq], inc="vS")
        ev = E.op("vector", lambda e: e.tensor_tensor(
            v4[:, sl], q4[:, sl], v4[:, sl], ALU.subtract),
            waits=[ev], inc="vS")
        ws = [ev] + ([fence_s] if fence_s else [])
        ev = E.op("scalar", lambda e: e.activation(
            rs4[:, sl], v4[:, sl], AF.Ln, bias=eps[:, 0:1]),
            waits=ws, inc="sS")
        ev = E.op("scalar", lambda e: e.activation(
            rs4[:, sl], rs4[:, sl], AF.Exp, scale=-0.5), waits=[ev], inc="sS")
        ev_nm = E.op("vector", lambda e: e.scalar_tensor_tensor(
            nm4[:, sl], m4[:, sl], -1.0, rs4[:, sl], ALU.mult, ALU.mult),
            waits=[ev], inc="vS")
        ev_d = None
        for t in range(4):
            aw = [ev_nm]
            if ev_d:
                aw.append(ev_d)
            else:
                aw.append(ev_id4)
                if gc - 1 in tr_last_c:
                    aw.append(tr_last_c[gc - 1])
            ev_d = E.op("scalar", lambda e, t=t: e.activation(
                rds[:, t * 128:(t + 1) * 128], ident[:, :], AF.Identity,
                scale=rs4[:, p4 + t:p4 + t + 1]), waits=aw, inc="sS")
            ev_d = E.op("scalar", lambda e, t=t: e.activation(
                rdn[:, t * 512:(t + 1) * 512], idb4[:, :], AF.Identity,
                scale=nm4[:, p4 + t:p4 + t + 1]), waits=[ev_d], inc="sS")
            ev_diag_c[(gc, t)] = ev_d
        ln4_war[gc % 2] = (ev_d, ev_nm)
        stats_evs[gc] = (ev_red, ev_sq)

    def c_pass1(gc):
        ev_tr = None
        for t in range(4):
            xta = x2lt[:, t * H:(t + 1) * H]
            evs = []
            for b in range(2):
                bank = ctr_banks[b % 2]
                wtr = [ev_diag_c[(gc, t)], ev_ones]
                if t == 0 and b == 0 and cx2f_free[0]:
                    wtr.append(cx2f_free[0])
                first_waits = wtr + list(bank.r_evs)
                for idx in range(4):
                    j = 4 * b + idx

                    def fn(e, idx=idx, j=j, bank=bank, xta=xta, t=t):
                        return e.matmul(bank.ap[:, idx * 128:(idx + 1) * 128],
                                        xta[:, j * 128:(j + 1) * 128],
                                        rds[:, t * 128:(t + 1) * 128],
                                        start=(idx == 0), stop=False)
                    E.op("tensor", fn, waits=first_waits if idx == 0 else ())

                def fnw(e, bank=bank, t=t):
                    return e.matmul(bank.ap[:, 0:512], onesb[:, :],
                                    rdn[:, t * 512:(t + 1) * 512],
                                    start=False, stop=True)
                ev_tr = E.op("tensor", fnw, inc="tS")
                bank.w_ev = ev_tr
                bank.r_evs = []
                dstf = x2f[:, :].rearrange("p (c t) -> p c t", t=512)[
                    :, 4 * b:4 * b + 4, t * 128:(t + 1) * 128]
                srcb = bank.ap.rearrange("p (c t) -> p c t", c=4)
                ev = read_bank("vector", [bank],
                               lambda e, d=dstf, s=srcb: e.tensor_copy(d, s),
                               inc="vS")
                evs.append(ev)
            ev_x2f_evac[(gc, t)] = evs
        tr_last_c[gc] = ev_tr
        x2lt_readers[gc] = [ev_tr, stats_evs[gc][0], stats_evs[gc][1]]

    def c_pass2(gc):
        for t in range(4):
            qt = gc * 4 + t
            bank = rt_banks[t % 2]
            mms = []
            for hc in range(HC):
                mms.append((
                    (lambda b=bank: b.ap[:, 0:8]),
                    (lambda hc=hc, t=t: x2f[:, hc * 512 + t * 128: hc * 512 + t * 128 + 128]),
                    (lambda hc=hc: rwb[:, hc * 8:(hc + 1) * 8]),
                ))
            mm_group(bank, mms, waits=list(ev_x2f_evac[(gc, t)]) + [ev_rw])
            r8 = rt8[:, t * 8:(t + 1) * 8]
            wrt = list(rt_slice_prev.get(t, ()))
            ev_rt = read_bank("scalar", [bank],
                              lambda e, d=r8, b=bank: e.copy(d, b.ap[:, 0:8]),
                              waits=wrt, inc="sS")
            mxr = rsm[:, 0:1]
            nbr = rsm[:, 1:2]
            smr = rsm[:, 2:3]
            rir = rsm[:, 3:4]
            ws = [ev_rt]
            if rt_state["prev_v"]:
                ws.append(rt_state["prev_v"])
            if rt_state["prev_s"]:
                ws.append(rt_state["prev_s"])
            ev_mx = E.op("vector", lambda e, d=mxr, s=r8:
                         e.reduce_max(d, s, axis=AX), waits=ws, inc="vS")
            ev_nb = E.op("vector", lambda e: e.tensor_scalar(
                nbr, mxr, -1.0, None, ALU.mult), waits=[ev_mx], inc="vS")
            sws = [ev_nb, ev_rt]
            if rt_state["prev_s"]:
                sws.append(rt_state["prev_s"])
            ev_pr = E.op("scalar",
                         lambda e, s=r8: e.activation(prb8[:, :], s,
                                                      AF.Exp, bias=nbr,
                                                      accum_out=smr),
                         waits=sws, inc="sS")
            rt_slice_prev[t] = (ev_mx, ev_pr)
            rt_state["prev_s"] = ev_pr
            ev = E.op("vector", lambda e: e.reciprocal(rir, smr),
                      waits=[ev_pr], inc="vS")
            ev = E.op("vector", lambda e: e.tensor_scalar(
                pn8[:, :], prb8[:, :], rir, None, ALU.mult),
                waits=[ev], inc="vS")
            ev = E.op("vector", lambda e: e.max(t8[:, :], pn8[:, :]),
                      waits=[ev], inc="vS")
            ev = E.op("vector", lambda e: e.tensor_tensor(
                smr, t8[:, 0:1], t8[:, 1:2], ALU.add), waits=[ev], inc="vS")
            ev = E.op("vector", lambda e: e.reciprocal(rir, smr),
                      waits=[ev], inc="vS")
            ev = E.op("vector", lambda e: e.tensor_tensor(
                mxr, t8[:, 0:1], rir, ALU.mult), waits=[ev], inc="vS")
            ev = E.op("vector", lambda e: e.tensor_tensor(
                nbr, t8[:, 1:2], rir, ALU.mult), waits=[ev], inc="vS")
            ev_sc[qt] = E.op("vector", lambda e, t=t: e.tensor_tensor(
                sc4[:, t:t + 1], mxr, nbr, ALU.add), waits=[ev], inc="vS")
            rt_state["prev_v"] = ev_sc[qt]

    c_stats(0)
    c_pass1(0)
    emit_w2()
    for gc in range(4):
        x2f_ready = [e for t in range(4) for e in ev_x2f_evac[(gc, t)]]
        for fc in range(NFC):
            bank = ffn_banks[fc % 3]
            mms = []
            for hc in range(HC):
                mms.append((
                    (lambda b=bank: b.ap),
                    (lambda hc=hc, fc=fc: w1b[:, hc * FF + fc * 128: hc * FF + fc * 128 + 128]),
                    (lambda hc=hc: x2f[:, hc * 512:(hc + 1) * 512]),
                ))
            mm_group(bank, mms, waits=x2f_ready + [ev_w1[fc // 4], ev_b1])
            dstg = h1g[:, fc * 512:(fc + 1) * 512]
            ev = read_bank("scalar", [bank],
                           lambda e, d=dstg, s=bank.ap, fc=fc:
                           e.activation(d, s, AF.Gelu_apprx_tanh,
                                        bias=b1f[:, fc:fc + 1]),
                           inc="sS")
            ev_gelu[(gc, fc)] = ev
            if fc == 6:
                c_pass2(gc)
        cx2f_free[0] = E.ev("tS")
        if gc + 1 < 4:
            c_stats(gc + 1)
        for k in range(4):
            qt = gc * 4 + k
            wxr = []
            if qt - 2 >= 0:
                wxr.append(ev_yadd[qt - 2])
            ev_xr = E.op("sync", lambda e, d=x2rt[:, (k % 2) * H:(k % 2 + 1) * H],
                         t_=qt: e.dma_start(d, x2s[t_]), waits=wxr, inc="dSy")
            bkL, bkR = (BK[5], BK[6]) if k % 2 == 0 else (BK[0], BK[1])
            full = pair_ap(5) if k % 2 == 0 else pair_ap(0)
            mmsL, mmsR = [], []
            for fc in range(NFC):
                lhs = (lambda fc=fc, k=k: h1g[:, fc * 512 + k * 128: fc * 512 + k * 128 + 128])
                mmsL.append(((lambda b=bkL: b.ap), lhs,
                             (lambda fc=fc: w2b[:, fc * H: fc * H + 512])))
                mmsR.append(((lambda b=bkR: b.ap), lhs,
                             (lambda fc=fc: w2b[:, fc * H + 512: fc * H + 1024])))
            mm_group_pair(bkL, bkR, mmsL, mmsR,
                          waits=[ev_gelu[(gc, NFC - 1)]] + list(ev_w2))
            wcp = []
            if qt == 0:
                wcp.append(ev_x2store[NQT - 2])
            if qt - 1 >= 0 and qt - 1 in ev_yadd:
                wcp.append(ev_yadd[qt - 1])
            ev_cp = read_bank("scalar", [bkL, bkR],
                              lambda e, f=full: e.copy(eot[:, :], f),
                              waits=wcp, inc="sS")
            ev_t1 = E.op("gpsimd", lambda e: e.tensor_tensor(
                eot[:, :], eot[:, :], b2bc[:, :], ALU.add),
                waits=[ev_cp, ev_b2], inc="gS")
            ev_t2 = E.op("scalar", lambda e, k=k: e.activation(
                eot[:, :], eot[:, :], AF.Identity, scale=sc4[:, k:k + 1]),
                waits=[ev_t1, ev_sc[qt]], inc="sS")
            ev_nm, rs_ap, nm_ap = ln_stats(qt, lambda: eot[:, :],
                                           lambda: sqc[:, :],
                                           stat_waits_v=[ev_t2],
                                           stat_waits_s=[ev_t2],
                                           sq_cell=csq_prev)
            ev_ln = E.op("scalar", lambda e, rs_=rs_ap, nm_=nm_ap: e.activation(
                eot[:, :], eot[:, :], AF.Identity, bias=nm_, scale=rs_),
                waits=[ev_nm], inc="sS")
            ln_register_apply(qt, ev_ln)
            wadd = [ev_ln, ev_xr]
            if qt == 0:
                wadd.append(ev_x2store[NQT - 1])
            if qt - 1 >= 0 and qt - 1 in ev_ystore:
                wadd.append(ev_ystore[qt - 1])
            ev_ya = E.op("gpsimd", lambda e, k=k: e.tensor_tensor(
                yt[:, :], eot[:, :], x2rt[:, (k % 2) * H:(k % 2 + 1) * H],
                ALU.add), waits=wadd, inc="gS")
            ev_yadd[qt] = ev_ya
            ev_ystore[qt] = E.op("sync", lambda e, t_=qt:
                                 e.dma_start(y[t_ * 128:(t_ + 1) * 128, :],
                                             yt[:, :]),
                                 waits=[ev_ya], inc="dSy")
        if gc + 1 < 4:
            c_pass1(gc + 1)

    E.op("gpsimd", lambda e: e.nop(),
         waits=[(f"dGp{i}", E.counts[f"dGp{i}"]) for i in range(NDMA)])
    E.op("sync", lambda e: e.nop(),
         waits=[(f"dSy{i}", E.counts[f"dSy{i}"]) for i in range(NDMA)])

    semctx = contextlib.ExitStack()
    sems = {s: semctx.enter_context(nc.semaphore(s)) for s in SEM_NAMES}
    with nc.Block("init") as b0:
        def _clr(e):
            for s_ in sems.values():
                e.sem_clear(s_)
        b0.gpsimd(_clr)
    with nc.Block() as block:
        E.replay(block, sems)
    semctx.close()
    c_es.close()
    es.close()
    return nc


def _prep_inputs(L0_attn_x, L0_attn_W_qkv, L0_attn_W_out, L0_ffn_router_w,
                 L0_ffn_w1, L0_ffn_b1, L0_ffn_w2, L0_ffn_b2):
    f32 = np.float32
    x = np.ascontiguousarray(np.asarray(L0_attn_x, dtype=f32))
    Wqkv = np.asarray(L0_attn_W_qkv, dtype=f32)
    Wout = np.asarray(L0_attn_W_out, dtype=f32)
    rw = np.asarray(L0_ffn_router_w, dtype=f32)
    w1 = np.ascontiguousarray(np.asarray(L0_ffn_w1, dtype=f32))
    w2 = np.ascontiguousarray(np.asarray(L0_ffn_w2, dtype=f32))
    b1 = np.asarray(L0_ffn_b1, dtype=f32)
    b2 = np.asarray(L0_ffn_b2, dtype=f32)

    wq_ = np.ascontiguousarray(Wqkv.T[:, 0::3])
    wk_ = np.ascontiguousarray(Wqkv.T[:, 1::3])
    wv_ = np.ascontiguousarray(Wqkv.T[:, 2::3])
    wo_ = np.ascontiguousarray(Wout.T)
    b1r = np.ascontiguousarray(b1.reshape(32, 128).T)
    b2t = np.ascontiguousarray(np.broadcast_to(b2, (128, H)))
    rwr = np.ascontiguousarray(rw.reshape(8, 128, 8).transpose(1, 0, 2)
                               .reshape(128, 64))
    idb = np.eye(128, dtype=ml_dtypes.bfloat16)

    shared = dict(wq=wq_, wk=wk_, wv=wv_, wo=wo_, w1=w1, w2=w2,
                  b1r=b1r, b2t=b2t, rwr=rwr, idb=idb)
    in_maps = []
    for c in range(8):
        b, h = c // 2, c % 2
        m = dict(shared)
        m["xo"] = np.ascontiguousarray(x[b, h * TQ:(h + 1) * TQ, :])
        m["xr"] = np.ascontiguousarray(x[b, (1 - h) * TQ:(2 - h) * TQ, :])
        in_maps.append(m)
    return in_maps


def kernel(L0_attn_x, L0_attn_W_qkv, L0_attn_W_out, L0_ffn_router_w,
           L0_ffn_w1, L0_ffn_b1, L0_ffn_w2, L0_ffn_b2):
    in_maps = _prep_inputs(L0_attn_x, L0_attn_W_qkv, L0_attn_W_out,
                           L0_ffn_router_w, L0_ffn_w1, L0_ffn_b1,
                           L0_ffn_w2, L0_ffn_b2)
    nc = build_program()
    res = run_bass_kernel_spmd(nc, in_maps, core_ids=list(range(8)))
    out = np.empty((4, 4096, 1024), dtype=np.float32)
    for c in range(8):
        b, h = c // 2, c % 2
        out[b, h * TQ:(h + 1) * TQ, :] = res.results[c]["y"]
    return out


# version 29 (final)
# speedup vs baseline: 1.0326x; 1.0326x over previous
import contextlib

import numpy as np
import ml_dtypes

import concourse.bass as bass
import concourse.mybir as mybir
from concourse.bass_utils import run_bass_kernel_spmd

F32 = mybir.dt.float32
BF16 = mybir.dt.bfloat16
AF = mybir.ActivationFunctionType
ALU = mybir.AluOpType
AX = mybir.AxisListType.X

H = 1024
HC = 8
TQ = 2048
NQT = 16
NQG = 4
TKV = 4096
NKT = 32
NKG = 8
FF = 4096
NFC = 32

ENGINES = ("sync", "gpsimd", "vector", "scalar", "tensor")
NDMA = 8

SEM_NAMES = (["tS", "vS", "sS", "gS"]
             + [f"dSy{i}" for i in range(NDMA)]
             + [f"dGp{i}" for i in range(NDMA)])


class Em:

    def __init__(self):
        self.streams = {e: [] for e in ENGINES}
        self.counts = {n: 0 for n in SEM_NAMES}
        self.qcount = {"dSy": 0, "dGp": 0}

    def ev(self, name):
        return (name, self.counts[name])

    def op(self, eng, fn, waits=(), inc=None):
        if inc in ("dSy", "dGp"):
            pool = inc
            idx = self.qcount[pool]
            self.qcount[pool] += 1
            s = f"{pool}{idx % NDMA}"
            prior = idx // NDMA
            waits = list(waits) + [(s, 16 * prior)]
            self.counts[s] += 16
            ws = tuple((sn, v) for (sn, v) in waits if sn is not None and v > 0)
            self.streams[eng].append((ws, fn, s))
            return (s, self.counts[s])
        ws = tuple((sn, v) for (sn, v) in waits if sn is not None and v > 0)
        if inc is not None:
            self.counts[inc] += 1
        self.streams[eng].append((ws, fn, inc))
        return self.ev(inc) if inc else None

    def replay(self, block, sems):
        for eng_name in ENGINES:
            stream = self.streams[eng_name]

            def body(e, stream=stream):
                wm = {}
                for ws, fn, inc in stream:
                    for (s, v) in ws:
                        if v > wm.get(s, 0):
                            e.wait_ge(sems[s], v)
                            wm[s] = v
                    ins = fn(e)
                    if inc is not None:
                        ins.then_inc(sems[inc],
                                     16 if inc[0] == "d" else 1)

            getattr(block, eng_name)(body)


class Bank:

    def __init__(self, ap):
        self.ap = ap
        self.w_ev = None
        self.r_evs = []


def build_program(dbg=None, qhack=False, bstop=None):
    nc = bass.Bass("TRN2", target_bir_lowering=False)

    xo = nc.dram_tensor("xo", [TQ, H], F32, kind="ExternalInput").ap()
    xr = nc.dram_tensor("xr", [TQ, H], F32, kind="ExternalInput").ap()
    wq = nc.dram_tensor("wq", [H, H], F32, kind="ExternalInput").ap()
    wk = nc.dram_tensor("wk", [H, H], F32, kind="ExternalInput").ap()
    wv = nc.dram_tensor("wv", [H, H], F32, kind="ExternalInput").ap()
    wo = nc.dram_tensor("wo", [H, H], F32, kind="ExternalInput").ap()
    w1 = nc.dram_tensor("w1", [H, FF], F32, kind="ExternalInput").ap()
    w2 = nc.dram_tensor("w2", [FF, H], F32, kind="ExternalInput").ap()
    b1r = nc.dram_tensor("b1r", [128, 32], F32, kind="ExternalInput").ap()
    b2t = nc.dram_tensor("b2t", [128, H], F32, kind="ExternalInput").ap()
    rwr = nc.dram_tensor("rwr", [128, 64], F32, kind="ExternalInput").ap()
    idb = nc.dram_tensor("idb", [128, 128], BF16, kind="ExternalInput").ap()
    y = nc.dram_tensor("y", [TQ, H], F32, kind="ExternalOutput").ap()
    qs = nc.dram_tensor("qs", [NQG, 128, 8 * 512], BF16, kind="Internal").ap()
    x2s = nc.dram_tensor("x2s", [NQT, 128, H], F32, kind="Internal").ap()
    if dbg in ("A", "B"):
        dst_ = nc.dram_tensor("dst_", [128, 16], F32, kind="ExternalOutput").ap()
        dkf = nc.dram_tensor("dkf", [128, HC * TKV], BF16, kind="ExternalOutput").ap()
        dvt = nc.dram_tensor("dvt", [128, NKT * H], BF16, kind="ExternalOutput").ap()
        dqs = nc.dram_tensor("dqs", [NQG, 128, 8 * 512], BF16, kind="ExternalOutput").ap()
        dx2 = nc.dram_tensor("dx2", [NQT, 128, H], F32, kind="ExternalOutput").ap()

    E = Em()

    es = contextlib.ExitStack()
    S = lambda n, sh, dt: es.enter_context(nc.sbuf_tensor(n, sh, dt))
    PS = lambda n, sh: es.enter_context(nc.psum_tensor(n, sh, F32))

    ident = S("ident", [128, 128], BF16)
    eps = S("eps", [128, 1], F32)
    sm2 = S("sm2", [128, 2], F32)
    sq2 = S("sq2", [128, 2], F32)
    m2 = S("m2", [128, 2], F32)
    q2 = S("q2", [128, 2], F32)
    v2 = S("v2", [128, 2], F32)
    rstd = S("rstd", [128, 2], F32)
    nmb = S("nmb", [128, 2], F32)
    b1f = S("b1f", [128, 32], F32)
    rwb = S("rwb", [128, 64], BF16)
    sm4 = S("sm4", [128, 8], F32)
    sq4 = S("sq4", [128, 8], F32)
    m4 = S("m4", [128, 8], F32)
    q4 = S("q4", [128, 8], F32)
    v4 = S("v4", [128, 8], F32)
    rs4 = S("rs4", [128, 8], F32)
    nm4 = S("nm4", [128, 8], F32)
    dbgs = S("dbgs", [128, 16], F32)

    kv_es = contextlib.ExitStack()
    kf = kv_es.enter_context(nc.sbuf_tensor("kf", [128, HC * TKV], BF16))
    vt = kv_es.enter_context(nc.sbuf_tensor("vt", [128, NKT * H], BF16))

    pAll = PS("pAll", [128, 4096])
    BK = [Bank(pAll[:, i * 512:(i + 1) * 512]) for i in range(8)]

    def pair_ap(i):
        return pAll[:, i * 512:(i + 2) * 512]

    def mm_group(bank, mms, waits=()):
        n = len(mms)
        first_waits = list(waits) + list(bank.r_evs)
        ret = None
        for i, (o, l, r) in enumerate(mms):
            st = (i == 0)
            sp = (i == n - 1)

            def fn(e, o=o, l=l, r=r, st=st, sp=sp):
                return e.matmul(o(), l(), r(), start=st, stop=sp)

            w = first_waits if i == 0 else ()
            inc = "tS" if i == n - 1 else None
            ret = E.op("tensor", fn, waits=w, inc=inc) or ret
        bank.w_ev = ret
        bank.r_evs = []
        return ret

    def mm_group_pair(bankL, bankR, mmsL, mmsR, waits=()):
        n = len(mmsL)
        first_waits = list(waits) + list(bankL.r_evs) + list(bankR.r_evs)
        used_first = False
        ret = None
        for i in range(n):
            for which in (0, 1):
                mm = (mmsL, mmsR)[which][i]
                if mm is None:
                    continue
                o, l, r = mm
                st = (i == 0)
                sp = (i == n - 1)
                last = sp and (which == 1 or mmsR[i] is None)

                def fn(e, o=o, l=l, r=r, st=st, sp=sp):
                    return e.matmul(o(), l(), r(), start=st, stop=sp)

                w = () if used_first else first_waits
                inc = "tS" if last else None
                ev = E.op("tensor", fn, waits=w, inc=inc)
                used_first = True
                if ev:
                    ret = ev
        bankL.w_ev = ret
        bankR.w_ev = ret
        bankL.r_evs = []
        bankR.r_evs = []
        return ret

    def read_bank(eng, banks, fn, waits=(), inc=None):
        ws = list(waits)
        for b in banks:
            if b.w_ev is not None:
                ws.append(b.w_ev)
        ev = E.op(eng, fn, waits=ws, inc=inc)
        for b in banks:
            b.r_evs.append(ev)
        return ev

    def transpose_batch(bank, src_fn, js, waits=(), mov=None):
        n = len(js)
        first_waits = list(waits) + list(bank.r_evs)
        ret = None
        for idx, j in enumerate(js):
            def fn(e, idx=idx, j=j):
                m = ident[:, :] if mov is None else mov
                return e.matmul(bank.ap[:, idx * 128:(idx + 1) * 128],
                                src_fn(j), m, start=True, stop=True)
            w = first_waits if idx == 0 else ()
            inc = "tS" if idx == n - 1 else None
            ret = E.op("tensor", fn, waits=w, inc=inc) or ret
        bank.w_ev = ret
        bank.r_evs = []
        return ret

    ln_war = {0: None, 1: None}

    def ln_stats(i, src_fn, sq_fn, stat_waits_v=(), stat_waits_s=(),
                 src_banks=(), sq_cell=None, red_eng="vector"):
        p = i % 2
        sm = sm2[:, p:p + 1]
        sq = sq2[:, p:p + 1]
        m = m2[:, p:p + 1]
        q = q2[:, p:p + 1]
        v = v2[:, p:p + 1]
        rs = rstd[:, p:p + 1]
        nm = nmb[:, p:p + 1]
        fence = [ln_war[p]] if ln_war[p] else []
        if src_banks:
            ws = list(stat_waits_v) + fence
            for b_ in src_banks:
                if b_.w_ev is not None:
                    ws.append(b_.w_ev)
            ev_red = E.op("vector",
                          lambda e: e.reduce_sum(sm, src_fn(), axis=AX),
                          waits=ws, inc="vS")
            for b_ in src_banks:
                b_.r_evs.append(ev_red)
            ev_sq = read_bank("scalar", list(src_banks),
                              lambda e: e.activation(sq_fn(), src_fn(),
                                                     AF.Square, accum_out=sq),
                              waits=list(stat_waits_s) + fence, inc="sS")
        else:
            ev_red = E.op(red_eng,
                          lambda e: e.reduce_sum(sm, src_fn(), axis=AX),
                          waits=list(stat_waits_v) + fence,
                          inc="gS" if red_eng == "gpsimd" else "vS")
            sqw = list(stat_waits_s) + fence
            if sq_cell and sq_cell[0]:
                sqw.append(sq_cell[0])
            ev_sq = E.op("scalar",
                         lambda e: e.activation(sq_fn(), src_fn(), AF.Square,
                                                accum_out=sq),
                         waits=sqw, inc="sS")
            if sq_cell is not None:
                sq_cell[0] = ev_sq
        ev = E.op("scalar", lambda e: e.mul(m, sm, 1.0 / 1024),
                  waits=[ev_red, ev_sq], inc="sS")
        ev = E.op("scalar", lambda e: e.activation(v, m, AF.Square),
                  waits=[ev], inc="sS")
        ev = E.op("scalar", lambda e: e.mul(v, v, -1.0), waits=[ev], inc="sS")
        ev = E.op("scalar", lambda e: e.activation(q, sq, AF.Identity,
                                                   scale=1.0 / 1024, bias=v),
                  waits=[ev], inc="sS")
        ev = E.op("scalar", lambda e: e.activation(rs, q, AF.Ln,
                                                   bias=eps[:, 0:1]),
                  waits=[ev], inc="sS")
        ev = E.op("scalar", lambda e: e.activation(rs, rs, AF.Exp, scale=-0.5),
                  waits=[ev], inc="sS")
        ev = E.op("scalar", lambda e: e.mul(nm, m, -1.0), waits=[ev], inc="sS")
        ev_nm = E.op("scalar", lambda e: e.mul(nm, nm, rs), waits=[ev],
                     inc="sS")
        return ev_nm, rs, nm

    def ln_register_apply(i, ev_apply):
        ln_war[i % 2] = ev_apply

    def ln_chain(i, src_fn, out_fn, sq_fn, extra_stat_waits=(),
                 extra_apply_waits=()):
        ev_nm, rs, nm = ln_stats(i, src_fn, sq_fn,
                                 stat_waits_v=extra_stat_waits,
                                 stat_waits_s=extra_stat_waits)
        ev_apply = E.op("scalar",
                        lambda e: e.activation(out_fn(), src_fn(), AF.Identity,
                                               bias=nm, scale=rs),
                        waits=[ev_nm] + list(extra_apply_waits), inc="sS")
        ln_register_apply(i, ev_apply)
        return ev_apply

    def ln_psum_epilogue(qt, bankL, bankR, full_ap, out_fn, sq_fn,
                         extra_apply_waits=()):
        p = qt % 2
        fence = [ln_war[p]] if ln_war[p] else []
        ev_cp = read_bank("scalar", [bankL, bankR],
                          lambda e: e.copy(out_fn(), full_ap),
                          waits=fence + list(extra_apply_waits), inc="sS")
        ev_nm, rs, nm = ln_stats(qt, out_fn, sq_fn,
                                 stat_waits_v=[ev_cp], stat_waits_s=[ev_cp])
        ev_ao = E.op("scalar",
                     lambda e: e.activation(out_fn(), out_fn(), AF.Identity,
                                            bias=nm, scale=rs),
                     waits=[ev_nm], inc="sS")
        ln_register_apply(qt, ev_ao)
        return ev_ao

    E.op("vector", lambda e: e.memset(eps[:, :], 1e-5))
    ev_ident = E.op("sync", lambda e: e.dma_start(ident[:, :], idb[:, :]),
                    inc="dSy")
    ev_b1 = E.op("sync", lambda e: e.dma_start(b1f[:, :], b1r[:, :]),
                 inc="dSy")

    a_es = contextlib.ExitStack()
    SA = lambda n, sh, dt: a_es.enter_context(nc.sbuf_tensor(n, sh, dt))
    wqb = SA("wqb", [128, HC * H], BF16)
    wkb = SA("wkb", [128, HC * H], BF16)
    wvb = SA("wvb", [128, HC * H], BF16)
    xgb = SA("xgb", [128, 4 * H], BF16)
    xng = SA("xng", [128, 4 * H], BF16)
    xfg = SA("xfg", [128, HC * 512], BF16)
    qstage = SA("qstage", [128, 1024], BF16)
    sqa = SA("sqa", [128, 1024], BF16)

    def wld(dst, src):
        return E.op("gpsimd", lambda e, d=dst, s=src: e.dma_start(
            d[:, :].rearrange("p (c o) -> p c o", c=HC),
            s.rearrange("(c p) o -> p c o", p=128)), inc="dGp")

    ev_wq = wld(wqb, wq)
    ev_wk = wld(wkb, wk)
    ev_wv = wld(wvb, wv)
    ev_rw = E.op("gpsimd", lambda e: e.dma_start(rwb[:, :], rwr[:, :]),
                 inc="dGp")

    tbanks = [BK[3], BK[4]]
    pbanks = [BK[0], BK[1], BK[2], BK[7]]
    ev_apply = {}
    ev_evacT = {}
    ev_qs_store = {}
    qs_prev = [None]
    ev_qev = {}
    ev_vevac = {}
    ev_transp_last = {}
    last_xg_readers = {}

    ln4_war = {0: (None, None), 1: (None, None)}
    last_xgb_readers = {}
    ev_transp_g = {}
    sqa_prev = [None]
    ev_proj_g = {}

    ln_dma = {}

    def emit_ln_loads(g):
        src_x = xo if g < NQG else xr
        base = g * 512 if g < NQG else (g - NQG) * 512
        evs = []
        for t in range(4):
            row = base + t * 128
            dst = xgb[:, t * H:(t + 1) * H]
            w = list(last_xgb_readers.get(g - 1, []))
            evs.append(E.op("gpsimd", lambda e, r=row, s=src_x, d=dst:
                            e.dma_start(d, s[r:r + 128, :]), waits=w,
                            inc="dGp"))
        ln_dma[g] = evs

    def emit_ln_stats(g):
        p4 = (g % 2) * 4
        fence_s, fence_v = ln4_war[g % 2]
        ev_red = None
        ev_sq = sqa_prev[0]
        for t in range(4):
            ev_dma = ln_dma[g][t]
            dst = xgb[:, t * H:(t + 1) * H]
            wv = [ev_dma] + ([fence_v] if fence_v and t == 0 else [])
            if ev_sq:
                wv.append(ev_sq)
            ev_red = E.op("scalar", lambda e, t=t, d=dst:
                          e.activation(sqa[:, :], d, AF.Identity,
                                       accum_out=sm4[:, p4 + t:p4 + t + 1]),
                          waits=wv, inc="sS")
            ev_sq = ev_red
            ws = [ev_dma] + ([fence_s] if fence_s and t == 0 else [])
            if ev_sq:
                ws.append(ev_sq)
            ev_sq = E.op("scalar", lambda e, t=t, d=dst:
                         e.activation(sqa[:, :], d, AF.Square,
                                      accum_out=sq4[:, p4 + t:p4 + t + 1]),
                         waits=ws, inc="sS")
        sqa_prev[0] = ev_sq
        sl = slice(p4, p4 + 4)
        ev = E.op("vector", lambda e: e.tensor_scalar(
            m4[:, sl], sm4[:, sl], 1.0 / 1024, None, ALU.mult),
            waits=[ev_red], inc="vS")
        ev = E.op("vector", lambda e: e.tensor_tensor(
            v4[:, sl], m4[:, sl], m4[:, sl], ALU.mult), waits=[ev], inc="vS")
        ev = E.op("vector", lambda e: e.tensor_scalar(
            q4[:, sl], sq4[:, sl], 1.0 / 1024, None, ALU.mult),
            waits=[ev, ev_sq], inc="vS")
        ev = E.op("vector", lambda e: e.tensor_tensor(
            v4[:, sl], q4[:, sl], v4[:, sl], ALU.subtract),
            waits=[ev], inc="vS")
        ev = E.op("scalar", lambda e: e.activation(
            rs4[:, sl], v4[:, sl], AF.Ln, bias=eps[:, 0:1]),
            waits=[ev], inc="sS")
        ev = E.op("scalar", lambda e: e.activation(
            rs4[:, sl], rs4[:, sl], AF.Exp, scale=-0.5), waits=[ev], inc="sS")
        ev_nm = E.op("vector", lambda e: e.scalar_tensor_tensor(
            nm4[:, sl], m4[:, sl], -1.0, rs4[:, sl], ALU.mult, ALU.mult),
            waits=[ev], inc="vS")
        ev_ap = None
        for t in range(4):
            aw = [ev_nm]
            if ev_ap:
                aw.append(ev_ap)
            if g - 1 in ev_transp_g and t == 0:
                aw.append(ev_transp_g[g - 1])
            ev_ap = E.op("scalar", lambda e, t=t: e.activation(
                xng[:, t * H:(t + 1) * H], xgb[:, t * H:(t + 1) * H],
                AF.Identity, bias=nm4[:, p4 + t:p4 + t + 1],
                scale=rs4[:, p4 + t:p4 + t + 1]),
                waits=aw, inc="sS")
            ev_apply[4 * g + t] = ev_ap
        ln4_war[g % 2] = (ev_ap, ev_nm)
        last_xgb_readers[g] = [ev_ap, ev_red]

    def emit_transp_group(g):
        for t in range(4):
            i = 4 * g + t
            xnga = xng[:, t * H:(t + 1) * H]
            for b in range(2):
                bank = tbanks[(i * 2 + b) % 2]
                wtr = [ev_apply[i], ev_ident]
                if t == 0 and b == 0 and g - 1 in ev_proj_g:
                    wtr.append(ev_proj_g[g - 1])
                ev_tr = transpose_batch(
                    bank, lambda j, a=xnga: a[:, j * 128:(j + 1) * 128],
                    list(range(4 * b, 4 * b + 4)),
                    waits=wtr)
                if t == 3 and b == 1:
                    ev_transp_g[g] = ev_tr
                dst = xfg[:, :].rearrange("p (c t) -> p c t", t=512)[
                    :, 4 * b:4 * b + 4, t * 128:(t + 1) * 128]
                srcb = bank.ap.rearrange("p (c t) -> p c t", c=4)
                ev = read_bank("vector", [bank],
                               lambda e, d=dst, s=srcb: e.tensor_copy(d, s),
                               inc="vS")
                ev_evacT[(i, b)] = ev

    def proj_group(g, kind):
        wb = wqb if kind == "q" else wkb
        wev = ev_wq if kind == "q" else ev_wk
        xfg_ready = [ev_evacT[(g * 4 + t, b)] for t in range(4) for b in range(2)]
        for oc in range(HC):
            bank = pbanks[oc % 4]
            mms = []
            for hc in range(HC):
                mms.append((
                    (lambda bk=bank: bk.ap),
                    (lambda hc=hc, oc=oc, w=wb: w[:, hc * H + oc * 128: hc * H + oc * 128 + 128]),
                    (lambda hc=hc: xfg[:, hc * 512:(hc + 1) * 512]),
                ))
            mm_group(bank, mms, waits=xfg_ready + [wev])
            if kind == "k":
                dst = kf[:, oc * TKV + g * 512: oc * TKV + (g + 1) * 512]
                read_bank("vector", [bank],
                          lambda e, d=dst, s=bank.ap: e.tensor_copy(d, s),
                          inc="vS")
            else:
                half = oc % 2
                dst = qstage[:, half * 512:(half + 1) * 512]
                w2_ = [qs_prev[0]] if qs_prev[0] else []
                ev = read_bank("vector", [bank],
                               lambda e, d=dst, s=bank.ap: e.tensor_copy(d, s),
                               waits=w2_, inc="vS")
                ev_qev[half] = ev
                if half == 1:
                    ocp = oc // 2
                    dstq = qs[g][:, ocp * 1024:(ocp + 1) * 1024]
                    evs = E.op("gpsimd",
                               lambda e, d=dstq: e.dma_start(d, qstage[:, :]),
                               waits=[ev_qev[0], ev_qev[1]], inc="dGp")
                    ev_qs_store[(g, oc)] = evs
                    ev_qs_store[(g, oc - 1)] = evs
                    qs_prev[0] = evs

    def proj_v(g):
        xfg_ready = [ev_evacT[(g * 4 + t, b)] for t in range(4) for b in range(2)]
        for ts in range(4):
            tile = g * 4 + ts
            bkL, bkR = BK[5], BK[6]
            full = pair_ap(5)
            mmsL, mmsR = [], []
            for hc in range(HC):
                lhs = (lambda hc=hc, ts=ts: xfg[:, hc * 512 + ts * 128: hc * 512 + ts * 128 + 128])
                mmsL.append(((lambda b=bkL: b.ap), lhs,
                             (lambda hc=hc: wvb[:, hc * H: hc * H + 512])))
                mmsR.append(((lambda b=bkR: b.ap), lhs,
                             (lambda hc=hc: wvb[:, hc * H + 512: hc * H + 1024])))
            mm_group_pair(bkL, bkR, mmsL, mmsR, waits=xfg_ready + [ev_wv])
            dstL = vt[:, tile * H: tile * H + 512]
            dstR = vt[:, tile * H + 512:(tile + 1) * H]
            evL = read_bank("vector", [bkL],
                            lambda e, d=dstL, s=bkL.ap: e.tensor_copy(d, s),
                            inc="vS")
            evR = read_bank("scalar", [bkR],
                            lambda e, d=dstR, s=bkR.ap: e.copy(d, s),
                            inc="sS")
            ev_vevac[tile] = (evL, evR)

    def emit_proj(g):
        if g < NQG:
            proj_group(g, "q")
        proj_group(g, "k")
        proj_v(g)

    for g in range(NKG):
        emit_ln_loads(g)
        if g >= 1:
            if g - 1 < NQG:
                proj_group(g - 1, "q")
            emit_ln_stats(g)
            proj_group(g - 1, "k")
            proj_v(g - 1)
            ev_proj_g[g - 1] = E.ev("tS")
        else:
            emit_ln_stats(g)
        emit_transp_group(g)
    emit_proj(NKG - 1)

    ev_A_t = E.ev("tS")
    ev_A_v = E.ev("vS")
    ev_A_s = E.ev("sS")
    ev_A_end = [ev_A_t, ev_A_v, ev_A_s, ev_qs_store[(NQG - 1, 7)]]

    if dbg == "A":
        E.op("gpsimd", lambda e: e.dma_start(dst_[:, :], dbgs[:, :]),
             waits=[ev_A_v], inc="dGp")
        E.op("gpsimd", lambda e: e.dma_start(dkf[:, :], kf[:, :]), inc="dGp")
        E.op("gpsimd", lambda e: e.dma_start(dvt[:, :], vt[:, :]), inc="dGp")
        for g in range(NQG):
            E.op("gpsimd", lambda e, g=g: e.dma_start(dqs[g], qs[g]), inc="dGp")
        E.op("gpsimd", lambda e: e.nop(),
             waits=[(f"dGp{i}", E.counts[f"dGp{i}"]) for i in range(NDMA)])
        E.op("sync", lambda e: e.nop(),
             waits=[(f"dSy{i}", E.counts[f"dSy{i}"]) for i in range(NDMA)])
        semctx = contextlib.ExitStack()
        sems = {s: semctx.enter_context(nc.semaphore(s)) for s in SEM_NAMES}
        with nc.Block("init") as b0:
            def _clr(e):
                for s_ in sems.values():
                    e.sem_clear(s_)
            b0.gpsimd(_clr)
        with nc.Block() as block:
            E.replay(block, sems)
        semctx.close()
        a_es.close()
        kv_es.close()
        es.close()
        return nc

    a_es.close()
    b_es = contextlib.ExitStack()
    SB = lambda n, sh, dt: b_es.enter_context(nc.sbuf_tensor(n, sh, dt))
    wob = SB("wob", [128, HC * H], BF16)
    qtile = SB("qtile", [128, 2 * 1024], BF16)
    pbuf = SB("pbuf", [128, 2 * 4096], BF16)
    ptb = SB("ptb", [128, 4096], BF16)
    ctm = SB("ctm", [128, 1024], BF16)
    cfm = SB("cfm", [128, 1024], BF16)
    aot = SB("aot", [128, 2 * 1024], F32)
    sqb = SB("sqb", [128, 2 * 1024], BF16)
    xrt = SB("xrt", [128, 2 * 1024], F32)
    x2t = SB("x2t", [128, 2 * 1024], F32)
    diagb = SB("diagb", [128, 1024], BF16)
    m8 = SB("m8", [128, 16], F32)
    nb8 = SB("nb8", [128, 16], F32)
    l8 = SB("l8", [128, 16], F32)
    d8 = SB("d8", [128, 16], F32)
    e8 = SB("e8", [128, 16], F32)
    cf8 = SB("cf8", [128, 16], F32)
    lw8 = SB("lw8", [128, 16], F32)
    mg2 = SB("mg2", [128, 2], F32)
    lg2 = SB("lg2", [128, 2], F32)
    li2 = SB("li2", [128, 2], F32)

    E.op("vector", lambda e: e.nop(), waits=ev_A_end)
    E.op("scalar", lambda e: e.nop(), waits=ev_A_end)
    ev_wo = E.op("gpsimd", lambda e: e.dma_start(
        wob[:, :].rearrange("p (c o) -> p c o", c=HC),
        wo.rearrange("(c p) o -> p c o", p=128)),
        waits=ev_A_end, inc="dGp")

    sc_banks = [BK[0], BK[1], BK[2], BK[7]]
    tr_banks = [BK[3], BK[4]]
    ev_qdma = {}
    ev_xrt = {}
    ev_exp = {}
    ev_rescale = {}
    ev_diag = {}
    ev_transp_last = {}
    ev_transpPT = {}
    ev_ctm = {}
    ev_ctxT = {}
    ev_ao_mm = {}
    ev_aocp = {}
    ev_epi_ln = {}
    ev_evacCT = {}
    ev_add = {}
    ev_x2store = {}
    last_scores_stop = {}
    kf_free = {}
    vt_free = [None]
    w1b_ref = []
    ev_w1 = []
    gsub = [0]

    def emit_w1_chunk(ks):
        def fn(e, ks=ks):
            dstw = w1b_ref[0][:, :].rearrange(
                "p (c f) -> p c f", f=FF)[:, :, ks * 512:(ks + 1) * 512]
            srcw = w1[:, ks * 512:(ks + 1) * 512].rearrange(
                "(c p) f -> p c f", p=128)
            return e.dma_start(dstw, srcw)
        ev_w1.append(E.op("gpsimd", fn, waits=[kf_free[ks]], inc="dGp"))

    ev_kready = {g: list(ev_vevac[g * 4 + 3]) for g in range(NKG)}

    def emit_qdma(qt):
        g, k = qt // 4, qt % 4
        p = qt % 2
        w = [ev_qs_store[(g, 7)]]
        if qt == 0:
            w += ev_A_end
        if qt - 2 >= 0:
            w.append(last_scores_stop[qt - 2])
        dst = qtile[:, p * 1024:(p + 1) * 1024].rearrange(
            "p (c t) -> p c t", t=128)
        src = qs[g].rearrange("p (c t) -> p c t", t=512)[:, :, k * 128:(k + 1) * 128]
        if qhack:
            dflat = qtile[:, p * 1024:(p + 1) * 1024]
            sflat = qs[g][:, 0:1024]
            ev1 = E.op("sync", lambda e, d=dflat, s=sflat: e.dma_start(d, s),
                       waits=w, inc="dSy")
            ev_qdma[qt] = [ev1]
        else:
            ev1 = E.op("sync", lambda e, d=dst[:, 0:4], s=src[:, 0:4]:
                       e.dma_start(d, s), waits=w, inc="dSy")
            ev2 = E.op("sync", lambda e, d=dst[:, 4:8], s=src[:, 4:8]:
                       e.dma_start(d, s), inc="dSy")
            ev_qdma[qt] = [ev1, ev2]

    def emit_xrt(qt):
        p = qt % 2
        w = []
        if qt <= 1:
            w = list(ev_A_end)
        if qt - 2 >= 0 and qt - 2 in ev_add:
            w.append(ev_add[qt - 2])
        dst = xrt[:, p * 1024:(p + 1) * 1024]
        ev_xrt[qt] = E.op("gpsimd", lambda e, d=dst, r=qt * 128:
                          e.dma_start(d, xo[r:r + 128, :]), waits=w, inc="dGp")

    ev_rescale_l = {}
    ev_add_l = {}
    ev_aoev = {}

    def emit_scores(qt, ks0=0, ks1=8):
        p = qt % 2
        qa = qtile[:, p * 1024:(p + 1) * 1024]
        fence_v = list(filter(None, [ev_exp.get((qt - 2, 7)),
                                     ev_rescale.get(qt - 1)])) if ks0 == 0 else []
        fence_s = list(filter(None, [ev_rescale.get(qt - 2),
                                     ev_transp_last.get(qt - 2)])) if ks0 == 0 else []
        for ks in range(ks0, ks1):
            bank = sc_banks[gsub[0] % 4]
            gsub[0] += 1
            mms = []
            for hc in range(HC):
                mms.append((
                    (lambda b=bank: b.ap),
                    (lambda hc=hc, qa=qa: qa[:, hc * 128:(hc + 1) * 128]),
                    (lambda hc=hc, ks=ks: kf[:, hc * TKV + ks * 512: hc * TKV + (ks + 1) * 512]),
                ))
            ev_stop = mm_group(bank, mms, waits=ev_qdma[qt] + ev_kready[ks])
            if qt == NQT - 1:
                kf_free[ks] = ev_stop
            if ks == 7:
                last_scores_stop[qt] = ev_stop
            mxa = m8[:, p * 8 + ks: p * 8 + ks + 1]
            nba = nb8[:, p * 8 + ks: p * 8 + ks + 1]
            ws = [bank.w_ev] if bank.w_ev is not None else []
            ev_red = E.op("vector", lambda e, d=mxa, s=bank.ap:
                          e.reduce_max(d, s, axis=AX),
                          waits=ws + fence_v, inc="vS")
            fence_v = []
            bank.r_evs.append(ev_red)
            ev_nb = E.op("vector", lambda e, d=nba, s=mxa:
                         e.tensor_scalar(d, s, -0.03125, None, ALU.mult),
                         waits=[ev_red], inc="vS")
            dstp = pbuf[:, p * 4096 + ks * 512: p * 4096 + (ks + 1) * 512]
            acc = l8[:, p * 8 + ks: p * 8 + ks + 1]
            ev = read_bank("scalar", [bank],
                           lambda e, d=dstp, s=bank.ap, nb=nba, a=acc:
                           e.activation(d, s, AF.Exp, bias=nb, scale=0.03125,
                                        accum_out=a),
                           waits=[ev_nb] + fence_s, inc="sS")
            fence_s = []
            ev_exp[(qt, ks)] = ev

    def emit_softmax_tail(qt):
        if bstop == "scores":
            return
        p = qt % 2
        m8a = m8[:, p * 8:(p + 1) * 8]
        l8a = l8[:, p * 8:(p + 1) * 8]
        d8a = d8[:, p * 8:(p + 1) * 8]
        e8a = e8[:, p * 8:(p + 1) * 8]
        cfa = cf8[:, p * 8:(p + 1) * 8]
        lwa = lw8[:, p * 8:(p + 1) * 8]
        mga = mg2[:, p:p + 1]
        lga = lg2[:, p:p + 1]
        lia = li2[:, p:p + 1]
        ev = E.op("vector", lambda e: e.reduce_max(mga, m8a, axis=AX),
                  waits=[ev_exp[(qt, 7)]], inc="vS")
        ev_d = E.op("vector", lambda e: e.tensor_scalar(
            d8a, m8a, mga, None, ALU.subtract), waits=[ev], inc="vS")
        ev_e8 = E.op("scalar", lambda e: e.activation(
            e8a, d8a, AF.Exp, scale=0.03125), waits=[ev_d], inc="sS")
        ev = E.op("vector", lambda e: e.tensor_tensor(lwa, l8a, e8a, ALU.mult),
                  waits=[ev_e8], inc="vS")
        ev = E.op("vector", lambda e: e.reduce_sum(lga, lwa, axis=AX),
                  waits=[ev], inc="vS")
        ev = E.op("vector", lambda e: e.reciprocal(lia, lga), waits=[ev],
                  inc="vS")
        ev = E.op("vector", lambda e: e.tensor_scalar(cfa, e8a, lia, None,
                                                      ALU.mult),
                  waits=[ev], inc="vS")
        ev_rescale[qt] = ev
        wd = [ev]
        if qt - 1 in ev_transp_last:
            wd.append(ev_transp_last[qt - 1])
        for ks in range(8):
            da = diagb[:, ks * 128:(ks + 1) * 128]
            cfs = cfa[:, ks:ks + 1]
            ev_diag[(qt, ks)] = E.op("scalar", lambda e, d=da, s=cfs:
                                     e.activation(d, ident[:, :], AF.Identity,
                                                  scale=s),
                                     waits=wd, inc="sS")
            wd = []

    def emit_tail_transp(qt):
        p = qt % 2
        src_p = pbuf[:, p * 4096:(p + 1) * 4096]
        fence_pt = [ev_transpPT[(qt - 1, 7)]] if (qt - 1, 7) in ev_transpPT else []
        for b in range(8):
            bank = tr_banks[b % 2]
            ev_tr = transpose_batch(bank,
                            lambda j, s=src_p: s[:, j * 128:(j + 1) * 128],
                            list(range(4 * b, 4 * b + 4)),
                            waits=[ev_diag[(qt, b)]],
                            mov=diagb[:, b * 128:(b + 1) * 128])
            if b == 7:
                ev_transp_last[qt] = ev_tr
            dst = ptb[:, b * 512:(b + 1) * 512]
            ev = read_bank("vector", [bank],
                           lambda e, d=dst, s=bank.ap: e.tensor_copy(d, s),
                           waits=fence_pt, inc="vS")
            fence_pt = []
            ev_transpPT[(qt, b)] = ev

    def emit_tail_ctx(qt):
        p = qt % 2
        mmsL, mmsR = [], []
        for kb in range(NKT):
            lhs = (lambda kb=kb: ptb[:, kb * 128:(kb + 1) * 128])
            mmsL.append(((lambda: BK[5].ap), lhs,
                         (lambda kb=kb: vt[:, kb * H: kb * H + 512])))
            mmsR.append(((lambda: BK[6].ap), lhs,
                         (lambda kb=kb: vt[:, kb * H + 512: kb * H + 1024])))
        wctx = ([ev_transpPT[(qt, b)] for b in range(8)]
                + list(ev_vevac[NKT - 1]))
        ev_ctx_stop = mm_group_pair(BK[5], BK[6], mmsL, mmsR, waits=wctx)
        if qt == NQT - 1:
            vt_free[0] = ev_ctx_stop
        ctma = ctm[:, :]
        wctm = [ev_ctxT[qt - 1]] if qt - 1 in ev_ctxT else []
        ev_ctm[qt] = read_bank("scalar", [BK[5], BK[6]],
                               lambda e, d=ctma: e.copy(d, pair_ap(5)),
                               waits=wctm, inc="sS")

    def emit_tail_out(qt):
        p = qt % 2
        ctma = ctm[:, :]
        cfma = cfm[:, :]
        for b in range(2):
            bank = tr_banks[b % 2]
            ev_tr = transpose_batch(bank,
                            lambda j, s=ctma: s[:, j * 128:(j + 1) * 128],
                            list(range(4 * b, 4 * b + 4)),
                            waits=[ev_ctm[qt]])
            if b == 1:
                ev_ctxT[qt] = ev_tr
            wcf = [ev_ao_mm[qt - 1]] if (b == 0 and qt - 1 in ev_ao_mm) else []
            dst = cfma[:, b * 512:(b + 1) * 512]
            ev = read_bank("vector", [bank],
                           lambda e, d=dst, s=bank.ap: e.tensor_copy(d, s),
                           waits=wcf, inc="vS")
            ev_evacCT[(qt, b)] = ev
        mmsL, mmsR = [], []
        for hc in range(HC):
            lhs = (lambda hc=hc, c=cfma: c[:, hc * 128:(hc + 1) * 128])
            mmsL.append(((lambda: BK[3].ap), lhs,
                         (lambda hc=hc: wob[:, hc * H: hc * H + 512])))
            mmsR.append(((lambda: BK[4].ap), lhs,
                         (lambda hc=hc: wob[:, hc * H + 512: hc * H + 1024])))
        ev_mm = mm_group_pair(BK[3], BK[4], mmsL, mmsR,
                              waits=[ev_evacCT[(qt, 0)], ev_evacCT[(qt, 1)],
                                     ev_wo])
        ev_ao_mm[qt] = ev_mm
        aota = aot[:, p * 1024:(p + 1) * 1024]
        wcp = [ev_add[qt - 2]] if qt - 2 in ev_add else []
        ev_aocp[qt] = read_bank("scalar", [BK[3], BK[4]],
                                lambda e, d=aota: e.copy(d, pair_ap(3)),
                                waits=wcp, inc="sS")

    def emit_epi_ln(qt):
        p = qt % 2
        aota = aot[:, p * 1024:(p + 1) * 1024]
        ev_nm, rs_ap, nm_ap = ln_stats(qt, lambda a=aota: a,
                                       lambda: sqb[:, p * 1024:(p + 1) * 1024],
                                       stat_waits_v=[ev_aocp[qt]],
                                       stat_waits_s=[ev_aocp[qt]])
        ev_ln = E.op("scalar", lambda e, a=aota, rs_=rs_ap, nm_=nm_ap:
                     e.activation(a, a, AF.Identity, bias=nm_, scale=rs_),
                     waits=[ev_nm], inc="sS")
        ln_register_apply(qt, ev_ln)
        ev_epi_ln[qt] = ev_ln
        ev_aoev[qt] = ev_ln

    def emit_epi_add(qt):
        p = qt % 2
        aota = aot[:, p * 1024:(p + 1) * 1024]
        xra = xrt[:, p * 1024:(p + 1) * 1024]
        x2a = x2t[:, p * 1024:(p + 1) * 1024]
        wadd = [ev_epi_ln[qt], ev_xrt[qt]]
        if qt - 2 >= 0 and qt - 2 in ev_x2store:
            wadd.append(ev_x2store[qt - 2])
        ev_a = E.op("gpsimd", lambda e, d=x2a, a=aota, b=xra:
                    e.tensor_tensor(d, a, b, ALU.add), waits=wadd, inc="gS")
        ev_add[qt] = ev_a
        ev_add_l[qt] = ev_a
        ev_x2store[qt] = E.op("sync", lambda e, s=x2a, t=qt:
                              e.dma_start(x2s[t], s), waits=[ev_a], inc="dSy")

    emit_qdma(0)
    emit_xrt(0)
    emit_xrt(1)
    emit_scores(0)
    emit_softmax_tail(0)
    for qt in range(NQT):
        nxt = qt + 1 < NQT
        if nxt:
            emit_qdma(qt + 1)
            emit_scores(qt + 1, 0, 5)
            if qt == NQT - 2 and dbg is None:
                for hc in range(5):
                    emit_w1_chunk(hc)
        if qt - 2 >= 0:
            emit_epi_ln(qt - 2)
        emit_tail_transp(qt)
        if nxt:
            emit_scores(qt + 1, 5, 7)
            if qt == NQT - 2 and dbg is None:
                emit_w1_chunk(5)
                emit_w1_chunk(6)
        emit_tail_ctx(qt)
        if nxt:
            emit_scores(qt + 1, 7, 8)
            if qt == NQT - 2 and dbg is None:
                emit_w1_chunk(7)
        if qt - 2 >= 0:
            emit_epi_add(qt - 2)
            emit_xrt(qt)
        if not nxt:
            emit_epi_ln(qt - 1)
        emit_tail_out(qt)
        if nxt:
            emit_softmax_tail(qt + 1)
    emit_epi_add(NQT - 2)
    emit_epi_ln(NQT - 1)
    emit_epi_add(NQT - 1)

    ev_last_scores = last_scores_stop[NQT - 1]
    ev_B_t = E.ev("tS")
    ev_B_v = E.ev("vS")
    ev_B_s = E.ev("sS")

    if dbg == "B":
        wlast = list(ev_x2store.values())
        for t in range(NQT):
            E.op("gpsimd", lambda e, t=t: e.dma_start(dx2[t], x2s[t]),
                 waits=wlast, inc="dGp")
        E.op("gpsimd", lambda e: e.dma_start(dkf[:, :], kf[:, :]), inc="dGp")
        E.op("gpsimd", lambda e: e.dma_start(dvt[:, :], vt[:, :]), inc="dGp")
        for g in range(NQG):
            E.op("gpsimd", lambda e, g=g: e.dma_start(dqs[g], qs[g]), inc="dGp")
        E.op("gpsimd", lambda e: e.nop(),
             waits=[(f"dGp{i}", E.counts[f"dGp{i}"]) for i in range(NDMA)])
        E.op("sync", lambda e: e.nop(),
             waits=[(f"dSy{i}", E.counts[f"dSy{i}"]) for i in range(NDMA)])
        semctx = contextlib.ExitStack()
        sems = {s: semctx.enter_context(nc.semaphore(s)) for s in SEM_NAMES}
        with nc.Block("init") as b0:
            def _clr(e):
                for s_ in sems.values():
                    e.sem_clear(s_)
            b0.gpsimd(_clr)
        with nc.Block() as block:
            E.replay(block, sems)
        semctx.close()
        b_es.close()
        kv_es.close()
        es.close()
        return nc

    b_es.close()
    kv_es.close()
    c_es = contextlib.ExitStack()
    SC = lambda n, sh, dt: c_es.enter_context(nc.sbuf_tensor(n, sh, dt))
    w1b = SC("w1b", [128, HC * FF], BF16)
    w2b = SC("w2b", [128, NFC * H], BF16)
    h1g = SC("h1g", [128, NFC * 512], BF16)
    x2lt = SC("x2lt", [128, 4 * H], BF16)
    x2f = SC("x2f", [128, HC * 512], BF16)
    x2rt = SC("x2rt", [128, 2 * H], F32)
    b2bc = SC("b2bc", [128, H], F32)
    eot = SC("eot", [128, H], F32)
    yt = SC("yt", [128, H], F32)
    sqc = SC("sqc", [128, 1024], BF16)
    rt8 = SC("rt8", [128, 32], F32)
    rds = SC("rds", [128, 512], BF16)
    rdn = SC("rdn", [128, 4 * 512], BF16)
    onesb = SC("onesb", [128, 128], BF16)
    idb4 = SC("idb4", [128, 512], BF16)
    prb8 = SC("prb8", [128, 8], F32)
    pn8 = SC("pn8", [128, 8], F32)
    t8 = SC("t8", [128, 8], F32)
    rsm = SC("rsm", [128, 4], F32)
    sc4 = SC("sc4", [128, 4], F32)

    w1b_ref.append(w1b)
    ev_w2 = []

    def emit_w2():
        for r in range(4):
            srcw = w2[r * 1024:(r + 1) * 1024, :].rearrange(
                "(c p) o -> p c o", p=128)
            dstw = w2b[:, r * 8192:(r + 1) * 8192].rearrange(
                "p (c o) -> p c o", c=8)
            ev_w2.append(E.op("gpsimd",
                              lambda e, d=dstw, s=srcw: e.dma_start(d, s),
                              waits=[vt_free[0]], inc="dGp"))

    ev_b2 = E.op("sync", lambda e: e.dma_start(b2bc[:, :], b2t[:, :]),
                 waits=[ev_B_t, ev_B_v, ev_B_s, ev_x2store[NQT - 1]], inc="dSy")

    CB0 = [ev_B_t, ev_B_v, ev_B_s, ev_x2store[NQT - 2], ev_x2store[NQT - 1]]
    ev_ones = E.op("vector", lambda e: e.memset(onesb[:, :], 1.0),
                   waits=CB0, inc="vS")
    ev_id4 = ev_ones
    for j in range(4):
        ev_id4 = E.op("scalar", lambda e, j=j: e.copy(
            idb4[:, j * 128:(j + 1) * 128], ident[:, :]),
            waits=[ev_id4, ev_ident], inc="sS")

    ffn_banks = [BK[0], BK[1], BK[2]]
    ctr_banks = [BK[3], BK[4]]
    rt_banks = [BK[7], BK[5]]
    ev_gelu = {}
    ev_yadd = {}
    ev_ystore = {}
    ev_x2f_evac = {}
    ev_sc = {}
    csq_prev = [None]
    x2lt_readers = {}
    ev_ctr_g = {}
    cx2f_free = [None]
    rt_state = {"prev_v": None, "prev_s": None, "prev_mx": None}

    ev_diag_c = {}
    tr_last_c = {}
    stats_evs = {}
    rt_slice_prev = {}

    def c_stats(gc):
        p4 = (gc % 2) * 4
        fence_s, fence_v = ln4_war[gc % 2]
        ev_red = None
        ev_sq = csq_prev[0]
        for t in range(4):
            qt = gc * 4 + t
            dst = x2lt[:, t * H:(t + 1) * H]
            w = list(x2lt_readers.get(gc - 1, []))
            if gc == 0 and t == 0:
                w += [ev_B_t, ev_B_v, ev_B_s]
            w.append(ev_x2store[qt])
            ev_dma = E.op("gpsimd", lambda e, d=dst, t_=qt:
                          e.dma_start(d, x2s[t_]), waits=w, inc="dGp")
            wv = [ev_dma] + ([fence_v] if fence_v and t == 0 else [])
            ev_red = E.op("vector", lambda e, t=t, d=dst:
                          e.reduce_sum(sm4[:, p4 + t:p4 + t + 1], d, axis=AX),
                          waits=wv, inc="vS")
            ws = [ev_dma]
            if gc == 0 and t == 0:
                ws.append(ev_x2store[NQT - 2])
            if ev_sq:
                ws.append(ev_sq)
            ev_sq = E.op("scalar", lambda e, t=t, d=dst:
                         e.activation(sqc[:, :], d, AF.Square,
                                      accum_out=sq4[:, p4 + t:p4 + t + 1]),
                         waits=ws, inc="sS")
        csq_prev[0] = ev_sq
        sl = slice(p4, p4 + 4)
        ev = E.op("vector", lambda e: e.tensor_scalar(
            m4[:, sl], sm4[:, sl], 1.0 / 1024, None, ALU.mult),
            waits=[ev_red], inc="vS")
        ev = E.op("vector", lambda e: e.tensor_tensor(
            v4[:, sl], m4[:, sl], m4[:, sl], ALU.mult), waits=[ev], inc="vS")
        ev = E.op("vector", lambda e: e.tensor_scalar(
            q4[:, sl], sq4[:, sl], 1.0 / 1024, None, ALU.mult),
            waits=[ev, ev_sq], inc="vS")
        ev = E.op("vector", lambda e: e.tensor_tensor(
            v4[:, sl], q4[:, sl], v4[:, sl], ALU.subtract),
            waits=[ev], inc="vS")
        ws = [ev] + ([fence_s] if fence_s else [])
        ev = E.op("scalar", lambda e: e.activation(
            rs4[:, sl], v4[:, sl], AF.Ln, bias=eps[:, 0:1]),
            waits=ws, inc="sS")
        ev = E.op("scalar", lambda e: e.activation(
            rs4[:, sl], rs4[:, sl], AF.Exp, scale=-0.5), waits=[ev], inc="sS")
        ev_nm = E.op("vector", lambda e: e.scalar_tensor_tensor(
            nm4[:, sl], m4[:, sl], -1.0, rs4[:, sl], ALU.mult, ALU.mult),
            waits=[ev], inc="vS")
        ev_d = None
        for t in range(4):
            aw = [ev_nm]
            if ev_d:
                aw.append(ev_d)
            else:
                aw.append(ev_id4)
                if gc - 1 in tr_last_c:
                    aw.append(tr_last_c[gc - 1])
            ev_d = E.op("scalar", lambda e, t=t: e.activation(
                rds[:, t * 128:(t + 1) * 128], ident[:, :], AF.Identity,
                scale=rs4[:, p4 + t:p4 + t + 1]), waits=aw, inc="sS")
            ev_d = E.op("scalar", lambda e, t=t: e.activation(
                rdn[:, t * 512:(t + 1) * 512], idb4[:, :], AF.Identity,
                scale=nm4[:, p4 + t:p4 + t + 1]), waits=[ev_d], inc="sS")
            ev_diag_c[(gc, t)] = ev_d
        ln4_war[gc % 2] = (ev_d, ev_nm)
        stats_evs[gc] = (ev_red, ev_sq)

    def c_pass1(gc):
        ev_tr = None
        for t in range(4):
            xta = x2lt[:, t * H:(t + 1) * H]
            evs = []
            for b in range(2):
                bank = ctr_banks[b % 2]
                wtr = [ev_diag_c[(gc, t)], ev_ones]
                if t == 0 and b == 0 and cx2f_free[0]:
                    wtr.append(cx2f_free[0])
                first_waits = wtr + list(bank.r_evs)
                for idx in range(4):
                    j = 4 * b + idx

                    def fn(e, idx=idx, j=j, bank=bank, xta=xta, t=t):
                        return e.matmul(bank.ap[:, idx * 128:(idx + 1) * 128],
                                        xta[:, j * 128:(j + 1) * 128],
                                        rds[:, t * 128:(t + 1) * 128],
                                        start=(idx == 0), stop=False)
                    E.op("tensor", fn, waits=first_waits if idx == 0 else ())

                def fnw(e, bank=bank, t=t):
                    return e.matmul(bank.ap[:, 0:512], onesb[:, :],
                                    rdn[:, t * 512:(t + 1) * 512],
                                    start=False, stop=True)
                ev_tr = E.op("tensor", fnw, inc="tS")
                bank.w_ev = ev_tr
                bank.r_evs = []
                dstf = x2f[:, :].rearrange("p (c t) -> p c t", t=512)[
                    :, 4 * b:4 * b + 4, t * 128:(t + 1) * 128]
                srcb = bank.ap.rearrange("p (c t) -> p c t", c=4)
                ev = read_bank("vector", [bank],
                               lambda e, d=dstf, s=srcb: e.tensor_copy(d, s),
                               inc="vS")
                evs.append(ev)
            ev_x2f_evac[(gc, t)] = evs
        tr_last_c[gc] = ev_tr
        x2lt_readers[gc] = [ev_tr, stats_evs[gc][0], stats_evs[gc][1]]

    def c_pass2(gc):
        for t in range(4):
            qt = gc * 4 + t
            bank = rt_banks[t % 2]
            mms = []
            for hc in range(HC):
                mms.append((
                    (lambda b=bank: b.ap[:, 0:8]),
                    (lambda hc=hc, t=t: x2f[:, hc * 512 + t * 128: hc * 512 + t * 128 + 128]),
                    (lambda hc=hc: rwb[:, hc * 8:(hc + 1) * 8]),
                ))
            mm_group(bank, mms, waits=list(ev_x2f_evac[(gc, t)]) + [ev_rw])
            r8 = rt8[:, t * 8:(t + 1) * 8]
            wrt = list(rt_slice_prev.get(t, ()))
            ev_rt = read_bank("scalar", [bank],
                              lambda e, d=r8, b=bank: e.copy(d, b.ap[:, 0:8]),
                              waits=wrt, inc="sS")
            mxr = rsm[:, 0:1]
            nbr = rsm[:, 1:2]
            smr = rsm[:, 2:3]
            rir = rsm[:, 3:4]
            ws = [ev_rt]
            if rt_state["prev_v"]:
                ws.append(rt_state["prev_v"])
            if rt_state["prev_s"]:
                ws.append(rt_state["prev_s"])
            ev_mx = E.op("vector", lambda e, d=mxr, s=r8:
                         e.reduce_max(d, s, axis=AX), waits=ws, inc="vS")
            ev_nb = E.op("vector", lambda e: e.tensor_scalar(
                nbr, mxr, -1.0, None, ALU.mult), waits=[ev_mx], inc="vS")
            sws = [ev_nb, ev_rt]
            if rt_state["prev_s"]:
                sws.append(rt_state["prev_s"])
            ev_pr = E.op("scalar",
                         lambda e, s=r8: e.activation(prb8[:, :], s,
                                                      AF.Exp, bias=nbr,
                                                      accum_out=smr),
                         waits=sws, inc="sS")
            rt_slice_prev[t] = (ev_mx, ev_pr)
            rt_state["prev_s"] = ev_pr
            ev = E.op("vector", lambda e: e.reciprocal(rir, smr),
                      waits=[ev_pr], inc="vS")
            ev = E.op("vector", lambda e: e.tensor_scalar(
                pn8[:, :], prb8[:, :], rir, None, ALU.mult),
                waits=[ev], inc="vS")
            ev = E.op("vector", lambda e: e.max(t8[:, :], pn8[:, :]),
                      waits=[ev], inc="vS")
            ev = E.op("vector", lambda e: e.tensor_tensor(
                smr, t8[:, 0:1], t8[:, 1:2], ALU.add), waits=[ev], inc="vS")
            ev = E.op("vector", lambda e: e.reciprocal(rir, smr),
                      waits=[ev], inc="vS")
            ev = E.op("vector", lambda e: e.tensor_tensor(
                mxr, t8[:, 0:1], rir, ALU.mult), waits=[ev], inc="vS")
            ev = E.op("vector", lambda e: e.tensor_tensor(
                nbr, t8[:, 1:2], rir, ALU.mult), waits=[ev], inc="vS")
            ev_sc[qt] = E.op("vector", lambda e, t=t: e.tensor_tensor(
                sc4[:, t:t + 1], mxr, nbr, ALU.add), waits=[ev], inc="vS")
            rt_state["prev_v"] = ev_sc[qt]

    c_stats(0)
    c_pass1(0)
    emit_w2()
    for gc in range(4):
        x2f_ready = [e for t in range(4) for e in ev_x2f_evac[(gc, t)]]
        for fc in range(NFC):
            bank = ffn_banks[fc % 3]
            mms = []
            for hc in range(HC):
                mms.append((
                    (lambda b=bank: b.ap),
                    (lambda hc=hc, fc=fc: w1b[:, hc * FF + fc * 128: hc * FF + fc * 128 + 128]),
                    (lambda hc=hc: x2f[:, hc * 512:(hc + 1) * 512]),
                ))
            mm_group(bank, mms, waits=x2f_ready + [ev_w1[fc // 4], ev_b1])
            dstg = h1g[:, fc * 512:(fc + 1) * 512]
            ev = read_bank("scalar", [bank],
                           lambda e, d=dstg, s=bank.ap, fc=fc:
                           e.activation(d, s, AF.Gelu_apprx_tanh,
                                        bias=b1f[:, fc:fc + 1]),
                           inc="sS")
            ev_gelu[(gc, fc)] = ev
            if fc == 6:
                c_pass2(gc)
        cx2f_free[0] = E.ev("tS")
        if gc + 1 < 4:
            c_stats(gc + 1)
        for k in range(4):
            qt = gc * 4 + k
            wxr = []
            if qt - 2 >= 0:
                wxr.append(ev_yadd[qt - 2])
            ev_xr = E.op("sync", lambda e, d=x2rt[:, (k % 2) * H:(k % 2 + 1) * H],
                         t_=qt: e.dma_start(d, x2s[t_]), waits=wxr, inc="dSy")
            bkL, bkR = (BK[5], BK[6]) if k % 2 == 0 else (BK[0], BK[1])
            full = pair_ap(5) if k % 2 == 0 else pair_ap(0)
            mmsL, mmsR = [], []
            for fc in range(NFC):
                lhs = (lambda fc=fc, k=k: h1g[:, fc * 512 + k * 128: fc * 512 + k * 128 + 128])
                mmsL.append(((lambda b=bkL: b.ap), lhs,
                             (lambda fc=fc: w2b[:, fc * H: fc * H + 512])))
                mmsR.append(((lambda b=bkR: b.ap), lhs,
                             (lambda fc=fc: w2b[:, fc * H + 512: fc * H + 1024])))
            mm_group_pair(bkL, bkR, mmsL, mmsR,
                          waits=[ev_gelu[(gc, NFC - 1)]] + list(ev_w2))
            wcp = []
            if qt == 0:
                wcp.append(ev_x2store[NQT - 2])
            if qt - 1 >= 0 and qt - 1 in ev_yadd:
                wcp.append(ev_yadd[qt - 1])
            ev_cp = read_bank("scalar", [bkL, bkR],
                              lambda e, f=full: e.copy(eot[:, :], f),
                              waits=wcp, inc="sS")
            ev_t1 = E.op("gpsimd", lambda e: e.tensor_tensor(
                eot[:, :], eot[:, :], b2bc[:, :], ALU.add),
                waits=[ev_cp, ev_b2], inc="gS")
            ev_t2 = E.op("scalar", lambda e, k=k: e.activation(
                eot[:, :], eot[:, :], AF.Identity, scale=sc4[:, k:k + 1]),
                waits=[ev_t1, ev_sc[qt]], inc="sS")
            ev_nm, rs_ap, nm_ap = ln_stats(qt, lambda: eot[:, :],
                                           lambda: sqc[:, :],
                                           stat_waits_v=[ev_t2],
                                           stat_waits_s=[ev_t2],
                                           sq_cell=csq_prev)
            ev_ln = E.op("scalar", lambda e, rs_=rs_ap, nm_=nm_ap: e.activation(
                eot[:, :], eot[:, :], AF.Identity, bias=nm_, scale=rs_),
                waits=[ev_nm], inc="sS")
            ln_register_apply(qt, ev_ln)
            wadd = [ev_ln, ev_xr]
            if qt == 0:
                wadd.append(ev_x2store[NQT - 1])
            if qt - 1 >= 0 and qt - 1 in ev_ystore:
                wadd.append(ev_ystore[qt - 1])
            ev_ya = E.op("gpsimd", lambda e, k=k: e.tensor_tensor(
                yt[:, :], eot[:, :], x2rt[:, (k % 2) * H:(k % 2 + 1) * H],
                ALU.add), waits=wadd, inc="gS")
            ev_yadd[qt] = ev_ya
            ev_ystore[qt] = E.op("sync", lambda e, t_=qt:
                                 e.dma_start(y[t_ * 128:(t_ + 1) * 128, :],
                                             yt[:, :]),
                                 waits=[ev_ya], inc="dSy")
        if gc + 1 < 4:
            c_pass1(gc + 1)

    E.op("gpsimd", lambda e: e.nop(),
         waits=[(f"dGp{i}", E.counts[f"dGp{i}"]) for i in range(NDMA)])
    E.op("sync", lambda e: e.nop(),
         waits=[(f"dSy{i}", E.counts[f"dSy{i}"]) for i in range(NDMA)])

    semctx = contextlib.ExitStack()
    sems = {s: semctx.enter_context(nc.semaphore(s)) for s in SEM_NAMES}
    with nc.Block("init") as b0:
        def _clr(e):
            for s_ in sems.values():
                e.sem_clear(s_)
        b0.gpsimd(_clr)
    with nc.Block() as block:
        E.replay(block, sems)
    semctx.close()
    c_es.close()
    es.close()
    return nc


def _prep_inputs(L0_attn_x, L0_attn_W_qkv, L0_attn_W_out, L0_ffn_router_w,
                 L0_ffn_w1, L0_ffn_b1, L0_ffn_w2, L0_ffn_b2):
    f32 = np.float32
    x = np.ascontiguousarray(np.asarray(L0_attn_x, dtype=f32))
    Wqkv = np.asarray(L0_attn_W_qkv, dtype=f32)
    Wout = np.asarray(L0_attn_W_out, dtype=f32)
    rw = np.asarray(L0_ffn_router_w, dtype=f32)
    w1 = np.ascontiguousarray(np.asarray(L0_ffn_w1, dtype=f32))
    w2 = np.ascontiguousarray(np.asarray(L0_ffn_w2, dtype=f32))
    b1 = np.asarray(L0_ffn_b1, dtype=f32)
    b2 = np.asarray(L0_ffn_b2, dtype=f32)

    wq_ = np.ascontiguousarray(Wqkv.T[:, 0::3])
    wk_ = np.ascontiguousarray(Wqkv.T[:, 1::3])
    wv_ = np.ascontiguousarray(Wqkv.T[:, 2::3])
    wo_ = np.ascontiguousarray(Wout.T)
    b1r = np.ascontiguousarray(b1.reshape(32, 128).T)
    b2t = np.ascontiguousarray(np.broadcast_to(b2, (128, H)))
    rwr = np.ascontiguousarray(rw.reshape(8, 128, 8).transpose(1, 0, 2)
                               .reshape(128, 64))
    idb = np.eye(128, dtype=ml_dtypes.bfloat16)

    shared = dict(wq=wq_, wk=wk_, wv=wv_, wo=wo_, w1=w1, w2=w2,
                  b1r=b1r, b2t=b2t, rwr=rwr, idb=idb)
    in_maps = []
    for c in range(8):
        b, h = c // 2, c % 2
        m = dict(shared)
        m["xo"] = np.ascontiguousarray(x[b, h * TQ:(h + 1) * TQ, :])
        m["xr"] = np.ascontiguousarray(x[b, (1 - h) * TQ:(2 - h) * TQ, :])
        in_maps.append(m)
    return in_maps


def kernel(L0_attn_x, L0_attn_W_qkv, L0_attn_W_out, L0_ffn_router_w,
           L0_ffn_w1, L0_ffn_b1, L0_ffn_w2, L0_ffn_b2):
    in_maps = _prep_inputs(L0_attn_x, L0_attn_W_qkv, L0_attn_W_out,
                           L0_ffn_router_w, L0_ffn_w1, L0_ffn_b1,
                           L0_ffn_w2, L0_ffn_b2)
    nc = build_program()
    res = run_bass_kernel_spmd(nc, in_maps, core_ids=list(range(8)))
    out = np.empty((4, 4096, 1024), dtype=np.float32)
    for c in range(8):
        b, h = c // 2, c % 2
        out[b, h * TQ:(h + 1) * TQ, :] = res.results[c]["y"]
    return out
